# version 26
# baseline (speedup 1.0000x reference)
import contextlib

import numpy as np
import ml_dtypes

import concourse.bass as bass
import concourse.mybir as mybir
import concourse.bass_utils as bass_utils
from concourse.bass_utils import run_bass_kernel_spmd



BF = ml_dtypes.bfloat16
F32 = mybir.dt.float32
BF16 = mybir.dt.bfloat16
I32 = mybir.dt.int32
AF = mybir.ActivationFunctionType
ALU = mybir.AluOpType
AX = mybir.AxisListType.X

NEG = -1.0e30
PI = [0, 2, 4, 6, 1, 3, 5, 7]


ENGINES = ("tensor", "vector", "scalar", "gpsimd", "sync")
DMA_SEM = {"sync": "dS", "gpsimd": "dG", "scalar": "dA"}
ENG_SEM = {"tensor": "sP", "vector": "sV", "scalar": "sA", "gpsimd": "sG",
           "sync": "sS"}
NLANE = 6


def sem_key(t):
    if t.dma:
        return DMA_SEM[t.eng] + str(t.lane)
    return ENG_SEM[t.eng]


class Tok:
    __slots__ = ("eng", "dma", "val", "lane")

    def __init__(self, eng, dma, val, lane=0):
        self.eng = eng
        self.dma = dma
        self.val = val
        self.lane = lane


class Buf:
    __slots__ = ("name", "writer", "readers")

    def __init__(self, name):
        self.name = name
        self.writer = None
        self.readers = []


class Step:
    __slots__ = ("eng", "emit", "deps", "dma", "n_dma", "tok", "tag")


class Prog:
    def __init__(self):
        self.steps = {e: [] for e in ENGINES}
        self.dma_count = {e: [0] * NLANE for e in DMA_SEM}
        self.dma_seq = {e: 0 for e in DMA_SEM}
        self.last_dma_tok = {e: [None] * NLANE for e in DMA_SEM}
        self.comp_count = {e: 0 for e in ENGINES}
        self.pending_barrier = None
        self._barrier_seen = set()

    def add(self, eng, emit, reads=(), writes=(), deps=(), dma=False,
            n_dma=1, tag=""):
        dep_toks = list(deps)
        for b in reads:
            if b.writer is not None:
                dep_toks.append(b.writer)
        for b in writes:
            if b.writer is not None:
                dep_toks.append(b.writer)
            dep_toks.extend(b.readers)
        if self.pending_barrier is not None:
            dep_toks.extend(self.pending_barrier)
        st = Step()
        st.eng, st.emit, st.tag = eng, emit, tag
        st.dma = dma
        st.n_dma = n_dma if dma else 0
        if dma:
            lane = self.dma_seq[eng] % NLANE
            self.dma_seq[eng] += 1
            prev = self.last_dma_tok[eng][lane]
            if prev is not None:
                dep_toks.append(prev)
            self.dma_count[eng][lane] += st.n_dma
            tok = Tok(eng, True, 16 * self.dma_count[eng][lane], lane)
            self.last_dma_tok[eng][lane] = tok
        else:
            self.comp_count[eng] += 1
            tok = Tok(eng, False, self.comp_count[eng])
        st.deps = dep_toks
        st.tok = tok
        self.steps[eng].append(st)
        for b in reads:
            b.readers.append(tok)
        for b in writes:
            b.writer = tok
            b.readers = []
        if self.pending_barrier is not None:
            self._barrier_seen.add(eng)
            if self._barrier_seen == set(ENGINES):
                self.pending_barrier = None
        return tok

    def barrier(self):
        toks = [self.steps[e][-1].tok for e in ENGINES if self.steps[e]]
        for q in DMA_SEM:
            for t in self.last_dma_tok[q]:
                if t is not None:
                    toks.append(t)
        self.pending_barrier = toks
        self._barrier_seen = set()

    def verify(self):
        pos = {e: 0 for e in ENGINES}
        sem = {}

        def ok(t):
            return sem.get(sem_key(t), 0) >= t.val

        total = sum(len(s) for s in self.steps.values())
        done = 0
        progress = True
        while progress:
            progress = False
            for e in ENGINES:
                while pos[e] < len(self.steps[e]):
                    st = self.steps[e][pos[e]]
                    if not all(ok(t) for t in st.deps):
                        break
                    key = sem_key(st.tok)
                    sem[key] = sem.get(key, 0) + (16 * st.n_dma if st.dma
                                                  else 1)
                    pos[e] += 1
                    done += 1
                    progress = True
        if done != total:
            stuck = []
            for e in ENGINES:
                if pos[e] < len(self.steps[e]):
                    st = self.steps[e][pos[e]]
                    bad = [(t.eng, t.dma, t.val) for t in st.deps
                           if not ok(t)]
                    stuck.append((e, pos[e], st.tag, bad[:3]))
            raise RuntimeError(f"schedule deadlock: {done}/{total}; "
                               f"stuck={stuck[:4]}")

    def emit_all(self, nc, sems):
        self.verify()
        n_instr = [0]

        def make_body(ename):
            def body(eng):
                watermark = {}
                for st in self.steps[ename]:
                    for t in st.deps:
                        key = sem_key(t)
                        if watermark.get(key, 0) >= t.val:
                            continue
                        eng.wait_ge(sems[key], t.val)
                        watermark[key] = t.val
                        n_instr[0] += 1
                    if st.dma:
                        st.emit(eng, sems[sem_key(st.tok)])
                        n_instr[0] += st.n_dma
                    else:
                        last = st.emit(eng)
                        assert last is not None, f"empty step {st.tag}"
                        last.then_inc(sems[ENG_SEM[ename]], 1)
                        n_instr[0] += 1
            return body

        with nc.Block() as block:
            block.tensor(make_body("tensor"))
            block.vector(make_body("vector"))
            block.scalar(make_body("scalar"))
            block.gpsimd(make_body("gpsimd"))
            block.sync(make_body("sync"))
        return n_instr[0]



def make_cfg(T=2048, D=4096, H=32, I=11008):
    HD = 128
    assert D == H * HD and T % 128 == 0 and I % 128 == 0
    cfg = dict(T=T, D=D, H=H, I=I, HD=HD)
    cfg["NT"] = T // 128
    cfg["KD"] = D // 128
    cfg["JG"] = I // 128
    cfg["CH"] = min(512, T)
    cfg["TPC"] = cfg["CH"] // 128
    cfg["NCH"] = T // cfg["CH"]
    cfg["TC5"] = min(512, T)
    cfg["NT5"] = T // cfg["TC5"]
    cfg["JH"] = min(2048, D)
    cfg["NJH"] = D // cfg["JH"]
    cfg["PC"] = 16
    cfg["SJ"] = 8 * cfg["PC"]
    cfg["NST"] = I // cfg["SJ"]
    cfg["DNW"] = min(512, D)
    cfg["NC_DN"] = D // cfg["DNW"]
    cfg["TPR"] = min(4, cfg["NT"])
    cfg["NTR"] = cfg["NT"] // cfg["TPR"]
    return cfg


def host_consts(cfg):
    tri = np.zeros((128, 128), dtype=BF)
    for p in range(128):
        tri[p, p + 1:] = BF(NEG)
    ident = np.eye(128, dtype=BF)
    ones = np.ones((1, 512), dtype=BF)
    KD, JG = cfg["KD"], cfg["JG"]
    hot32 = np.zeros((KD, KD * 128), dtype=BF)
    for g in range(KD):
        hot32[g, g * 128:(g + 1) * 128] = 1
    hotJG = np.zeros((JG, JG * 128), dtype=BF)
    for g in range(JG):
        hotJG[g, g * 128:(g + 1) * 128] = 1
    conehot = np.zeros((128, KD * KD), dtype=BF)
    for g in range(KD):
        conehot[:, g * KD + g] = 1
    return dict(cident=ident, cones=ones, ctri=tri, chot32=hot32,
                chotJG=hotJG, cconehot=conehot)



def build_program(cfg):
    T, D, H, I, HD = cfg["T"], cfg["D"], cfg["H"], cfg["I"], cfg["HD"]
    NT, KD, JG = cfg["NT"], cfg["KD"], cfg["JG"]
    CH, TPC, NCH = cfg["CH"], cfg["TPC"], cfg["NCH"]
    TC5, NT5 = cfg["TC5"], cfg["NT5"]
    JH, NJH = cfg["JH"], cfg["NJH"]
    PC, SJ, NST = cfg["PC"], cfg["SJ"], cfg["NST"]
    DNW, NC_DN, TPR, NTR = cfg["DNW"], cfg["NC_DN"], cfg["TPR"], cfg["NTR"]

    nc = bass.Bass("TRN2", target_bir_lowering=False)

    d = {}
    d["x"] = nc.dram_tensor("x", [T, D], F32, kind="ExternalInput").ap()
    for w in ("wq", "wk", "wv"):
        d[w] = nc.dram_tensor(w, [H, 128, KD * 128], F32,
                              kind="ExternalInput").ap()
    d["wo"] = nc.dram_tensor("wo", [KD, 128, D], F32,
                             kind="ExternalInput").ap()
    d["bq"] = nc.dram_tensor("bq", [1, D], F32, kind="ExternalInput").ap()
    d["bk"] = nc.dram_tensor("bk", [1, D], F32, kind="ExternalInput").ap()
    d["bvc"] = nc.dram_tensor("bvc", [D, 1], F32, kind="ExternalInput").ap()
    d["bo"] = nc.dram_tensor("bo", [1, D], F32, kind="ExternalInput").ap()
    for m in ("g", "u"):
        d[m + "qw"] = nc.dram_tensor(m + "qw", [NST, 128, KD * PC], I32,
                                     kind="ExternalInput").ap()
        d[m + "sc"] = nc.dram_tensor(m + "sc", [NST, KD, SJ], F32,
                                     kind="ExternalInput").ap()
        d[m + "qz"] = nc.dram_tensor(m + "qz", [NST, KD, PC], I32,
                                     kind="ExternalInput").ap()
    d["dqw"] = nc.dram_tensor("dqw", [NC_DN, 128, JG * (DNW // 8)], I32,
                              kind="ExternalInput").ap()
    d["dsc"] = nc.dram_tensor("dsc", [NC_DN, JG, DNW], F32,
                              kind="ExternalInput").ap()
    d["dqz"] = nc.dram_tensor("dqz", [NC_DN, JG, DNW // 8], I32,
                              kind="ExternalInput").ap()
    d["cident"] = nc.dram_tensor("cident", [128, 128], BF16,
                                 kind="ExternalInput").ap()
    d["cones"] = nc.dram_tensor("cones", [1, 512], BF16,
                                kind="ExternalInput").ap()
    d["ctri"] = nc.dram_tensor("ctri", [128, 128], BF16,
                               kind="ExternalInput").ap()
    d["chot32"] = nc.dram_tensor("chot32", [KD, KD * 128], BF16,
                                 kind="ExternalInput").ap()
    d["cconehot"] = nc.dram_tensor("cconehot", [128, KD * KD], BF16,
                                   kind="ExternalInput").ap()
    d["chotJG"] = nc.dram_tensor("chotJG", [JG, JG * 128], BF16,
                                 kind="ExternalInput").ap()
    d["attnT"] = nc.dram_tensor("attnT", [H, 128, T], BF16).ap()
    d["x1"] = nc.dram_tensor("x1", [T, D], F32).ap()
    d["hT"] = nc.dram_tensor("hT", [JG, 128, T], BF16).ap()
    d["out"] = nc.dram_tensor("out", [T, D], F32, kind="ExternalOutput").ap()

    sems = {}
    for name in ENG_SEM.values():
        sems[name] = nc.alloc_semaphore(name)
    for name in DMA_SEM.values():
        for r in range(NLANE):
            sems[name + str(r)] = nc.alloc_semaphore(name + str(r))

    P = Prog()

    ident16 = nc.alloc_sbuf_tensor("ident16", [128, 128], BF16)
    ones16 = nc.alloc_sbuf_tensor("ones16", [1, 512], BF16)
    tri16 = nc.alloc_sbuf_tensor("tri16", [128, 128], BF16)
    LNW = max(T, JH)
    outer_es = contextlib.ExitStack()
    lnT = outer_es.enter_context(
        nc.sbuf_tensor("lnT", [128, KD * LNW], BF16))

    b_ident = Buf("ident16"); b_ones = Buf("ones16"); b_tri = Buf("tri16")
    b_lnT = Buf("lnT")

    def lnT3():
        return lnT[:].rearrange("p (k w) -> p k w", k=KD)

    def dma_step(eng_name, pairs, reads=(), writes=(), deps=(), tag=""):
        def emit(eng, dsem):
            for dst, src in pairs:
                eng.dma_start(dst, src).then_inc(dsem, 16)
        return P.add(eng_name, emit, reads=reads, writes=writes, deps=deps,
                     dma=True, n_dma=len(pairs), tag=tag)

    dma_step("sync", [(ident16[:], d["cident"]), (ones16[:], d["cones"]),
                      (tri16[:], d["ctri"])],
             writes=[b_ident, b_ones, b_tri], tag="consts")

    stack = contextlib.ExitStack()

    def emit_layernorm(src_ap, ph):
        with contextlib.ExitStack() as es:
            xst = [es.enter_context(
                nc.sbuf_tensor(f"xst{ph}_{i}", [128, D], F32))
                for i in range(3)]
            lnd = es.enter_context(nc.sbuf_tensor(f"lnd{ph}", [128, D], BF16))
            ln16 = [es.enter_context(
                nc.sbuf_tensor(f"ln16{ph}_{i}", [128, D], BF16))
                for i in range(2)]
            cols = es.enter_context(
                nc.sbuf_tensor(f"lncol{ph}", [128, 8], F32))
            p_ln = [es.enter_context(
                nc.psum_tensor(f"pln{ph}_{i}", [128, 2048], BF16))
                for i in range(2)]
            b_xst = [Buf("xst0"), Buf("xst1"), Buf("xst2")]
            b_lnd = Buf("lnd")
            b_ln16 = [Buf("ln16_0"), Buf("ln16_1")]
            b_cols = Buf("lncol")
            b_pln = [Buf("pln0"), Buf("pln1")]
            HKT = 2048 // 128

            for tc in range(NT):
                par = tc % 3
                x_t, bx = xst[par], b_xst[par]
                dma_step("scalar",
                         [(x_t[:], src_ap[tc * 128:(tc + 1) * 128, :])],
                         writes=[bx], tag=f"ln{ph}.x{tc}")
                P.add("vector",
                      lambda e, x_t=x_t: e.reduce_sum(cols[:, 0:1], x_t[:],
                                                      axis=AX),
                      reads=[bx], writes=[b_cols], tag=f"ln{ph}.sum{tc}")
                P.add("scalar",
                      lambda e, x_t=x_t: e.activation(
                          lnd[:], x_t[:], AF.Square, accum_out=cols[:, 1:2]),
                      reads=[bx], writes=[b_lnd, b_cols],
                      tag=f"ln{ph}.sq{tc}")
                P.add("vector",
                      lambda e: e.tensor_scalar(cols[:, 2:3], cols[:, 0:1],
                                                1.0 / D, 0.0, ALU.mult,
                                                ALU.add),
                      reads=[b_cols], writes=[b_cols], tag=f"ln{ph}.mean{tc}")
                P.add("vector",
                      lambda e: e.tensor_tensor(cols[:, 3:4], cols[:, 2:3],
                                                cols[:, 2:3], ALU.mult),
                      reads=[b_cols], writes=[b_cols], tag=f"ln{ph}.m2{tc}")
                P.add("vector",
                      lambda e: e.scalar_tensor_tensor(
                          cols[:, 4:5], cols[:, 1:2], 1.0 / D, cols[:, 3:4],
                          ALU.mult, ALU.subtract),
                      reads=[b_cols], writes=[b_cols], tag=f"ln{ph}.var{tc}")
                P.add("vector",
                      lambda e: e.tensor_scalar(cols[:, 4:5], cols[:, 4:5],
                                                1e-5, 1.0, ALU.add, ALU.mult),
                      reads=[b_cols], writes=[b_cols], tag=f"ln{ph}.eps{tc}")
                P.add("scalar",
                      lambda e: e.activation(cols[:, 5:6], cols[:, 4:5],
                                             AF.Ln),
                      reads=[b_cols], writes=[b_cols], tag=f"ln{ph}.lnv{tc}")
                P.add("scalar",
                      lambda e: e.activation(cols[:, 6:7], cols[:, 5:6],
                                             AF.Exp, scale=-0.5),
                      reads=[b_cols], writes=[b_cols], tag=f"ln{ph}.rstd{tc}")
                P.add("vector",
                      lambda e: e.scalar_tensor_tensor(
                          cols[:, 7:8], cols[:, 2:3], -1.0, cols[:, 6:7],
                          ALU.mult, ALU.mult),
                      reads=[b_cols], writes=[b_cols], tag=f"ln{ph}.nb{tc}")
                l16, bl16 = ln16[tc % 2], b_ln16[tc % 2]
                P.add("vector",
                      lambda e, x_t=x_t, l16=l16: e.tensor_scalar(
                          l16[:], x_t[:], cols[:, 6:7], cols[:, 7:8],
                          ALU.mult, ALU.add),
                      reads=[bx, b_cols], writes=[bl16],
                      tag=f"ln{ph}.norm{tc}")
                nbat = (KD + HKT - 1) // HKT
                for bi in range(nbat):
                    k0 = bi * HKT
                    k1 = min(KD, k0 + HKT)
                    pp, bpp = p_ln[bi % 2], b_pln[bi % 2]

                    def em_tr(e, l16=l16, k0=k0, k1=k1, pp=pp):
                        last = None
                        for k in range(k0, k1):
                            last = e.transpose(
                                pp[:, (k - k0) * 128:(k - k0 + 1) * 128],
                                l16[:, k * 128:(k + 1) * 128], ident16[:])
                        return last
                    P.add("tensor", em_tr, reads=[bl16, b_ident],
                          writes=[bpp], tag=f"ln{ph}.tr{tc}.{bi}")

                    def em_cp(e, k0=k0, k1=k1, pp=pp, tc=tc):
                        dst = lnT3()[:, k0:k1, tc * 128:(tc + 1) * 128]
                        return e.tensor_copy(dst, pp[:, 0:(k1 - k0) * 128])
                    P.add("vector", em_cp, reads=[bpp], writes=[b_lnT],
                          tag=f"ln{ph}.cp{tc}.{bi}")

    emit_layernorm(d["x"], 1)
    P.barrier()

    with contextlib.ExitStack() as es:
        w16 = {m: es.enter_context(
            nc.sbuf_tensor(f"w16{m}", [128, KD * 128], BF16))
            for m in ("q", "k", "v")}
        b_w16 = {m: Buf(f"w16{m}") for m in ("q", "k", "v")}
        bh16 = [es.enter_context(
            nc.sbuf_tensor(f"bh16_{i}", [1, 256], BF16)) for i in range(2)]
        b_bh = [Buf("bh0"), Buf("bh1")]
        bv_col = [es.enter_context(
            nc.sbuf_tensor(f"bv_col{i}", [128, 1], F32)) for i in range(2)]
        b_bvc = [Buf("bv0"), Buf("bv1")]
        qTr = [es.enter_context(
            nc.sbuf_tensor(f"qT{i}", [128, T], BF16)) for i in range(2)]
        kTr = [es.enter_context(
            nc.sbuf_tensor(f"kT{i}", [128, T], BF16)) for i in range(2)]
        a16v = [es.enter_context(
            nc.sbuf_tensor(f"a16v{i}", [128, T], BF16)) for i in range(2)]
        vNr = [es.enter_context(
            nc.sbuf_tensor(f"vN{i}", [128, NT * 128], BF16))
            for i in range(2)]
        b_qT = [Buf("qT0"), Buf("qT1")]
        b_kT = [Buf("kT0"), Buf("kT1")]
        b_vN = [Buf("vN0"), Buf("vN1")]
        b_a16v = [Buf("a16v0"), Buf("a16v1")]
        acol = [es.enter_context(
            nc.sbuf_tensor(f"acol{i}", [128, 8], F32)) for i in range(2)]
        diag16 = [es.enter_context(
            nc.sbuf_tensor(f"diag16_{i}", [128, 128], BF16))
            for i in range(2)]
        b_acol = [Buf("ac0"), Buf("ac1")]
        b_diag = [Buf("dg0"), Buf("dg1")]
        attTb = es.enter_context(
            nc.sbuf_tensor("attTb", [128, NT * CH], BF16))
        b_attTb = Buf("attTb")
        oT = es.enter_context(nc.sbuf_tensor("oT", [128, T], BF16))
        b_oT = Buf("oT")
        p_qkv = es.enter_context(nc.psum_tensor("p_qkv", [128, 512], F32))
        b_pqkv = Buf("p_qkv")
        p_sc = es.enter_context(
            nc.psum_tensor("p_sc", [128, min(T, 2048)], F32))
        b_psc = [Buf("p_sc_lo"), Buf("p_sc_hi")]
        p_tr = es.enter_context(nc.psum_tensor("p_tr", [128, 1024], F32))
        b_ptr = [Buf("p_tr0"), Buf("p_tr1")]
        p_av = es.enter_context(nc.psum_tensor("p_av", [128, 512], F32))
        b_pav = Buf("p_av")

        def attTb3():
            return attTb[:].rearrange("p (t c) -> p t c", t=NT)

        def fetch_head_w(h):
            for m, wname in (("q", "wq"), ("k", "wk"), ("v", "wv")):
                dma_step("gpsimd", [(w16[m][:], d[wname][h])],
                         writes=[b_w16[m]], tag=f"h{h}.w{m}")
            hp = h % 2
            dma_step("gpsimd",
                     [(bh16[hp][:, 0:128], d["bq"][:, h * 128:(h + 1) * 128]),
                      (bh16[hp][:, 128:256],
                       d["bk"][:, h * 128:(h + 1) * 128])],
                     writes=[b_bh[hp]], tag=f"h{h}.b")
            dma_step("sync",
                     [(bv_col[hp][:], d["bvc"][h * 128:(h + 1) * 128, :])],
                     writes=[b_bvc[hp]], tag=f"h{h}.bv")

        def qkv_units(h):
            hp = h % 2
            units = []
            for m, dstT, bdst, bias_sl, scale in (
                    ("q", qTr[hp], b_qT[hp], (0, 128),
                     1.0 / float(np.sqrt(HD))),
                    ("k", kTr[hp], b_kT[hp], (128, 256), 1.0),
                    ("v", a16v[hp], b_a16v[hp], None, 1.0)):
                for tch in range(NT5):
                    def unit(m=m, dstT=dstT, bdst=bdst, bias_sl=bias_sl,
                             scale=scale, tch=tch, h=h, hp=hp):
                        def em_mm(e):
                            last = None
                            for k in range(KD):
                                last = e.matmul(
                                    p_qkv[:, 0:TC5],
                                    w16[m][:, k * 128:(k + 1) * 128],
                                    lnT3()[:, k, tch * TC5:(tch + 1) * TC5],
                                    start=(k == 0),
                                    stop=(bias_sl is None and k == KD - 1))
                            if bias_sl is not None:
                                last = e.matmul(
                                    p_qkv[:, 0:TC5],
                                    bh16[hp][:, bias_sl[0]:bias_sl[1]],
                                    ones16[:, 0:TC5],
                                    start=False, stop=True)
                            return last
                        reads = [b_w16[m], b_lnT, b_ones]
                        if bias_sl is not None:
                            reads.append(b_bh[hp])
                        P.add("tensor", em_mm, reads=reads, writes=[b_pqkv],
                              tag=f"h{h}.{m}mm{tch}")
                        P.add("vector",
                              lambda e, dstT=dstT, tch=tch, scale=scale:
                              e.tensor_scalar(
                                  dstT[:, tch * TC5:(tch + 1) * TC5],
                                  p_qkv[:, 0:TC5], scale, 0.0,
                                  ALU.mult, ALU.add),
                              reads=[b_pqkv], writes=[bdst],
                              tag=f"h{h}.{m}ev{tch}")
                    units.append(unit)
            for bi in range((NT + 3) // 4):
                def unit(bi=bi, h=h, hp=hp):
                    i0v, i1v = bi * 4, min(NT, bi * 4 + 4)
                    half = bi % 2
                    bptr = b_ptr[half]

                    def em_vtr(e):
                        last = None
                        for i in range(i0v, i1v):
                            last = e.matmul(
                                p_tr[:, half * 512 + (i - i0v) * 128:
                                     half * 512 + (i - i0v + 1) * 128],
                                a16v[hp][:, i * 128:(i + 1) * 128],
                                ident16[:], start=True, stop=True)
                        return last
                    P.add("tensor", em_vtr, reads=[b_a16v[hp], b_ident],
                          writes=[bptr], tag=f"h{h}.vtr{bi}")

                    def em_vcp(e):
                        return e.tensor_copy(
                            vNr[hp][:, i0v * 128:i1v * 128],
                            p_tr[:, half * 512:half * 512 +
                                 (i1v - i0v) * 128])
                    P.add("vector", em_vcp, reads=[bptr],
                          writes=[b_vN[hp]], tag=f"h{h}.vcp{bi}")
                units.append(unit)
            return units

        fetch_head_w(0)
        for u in qkv_units(0):
            u()
        for h in range(H):
            hp = h % 2
            qT, kT, vN = qTr[hp], kTr[hp], vNr[hp]
            bqT, bkT, bvN = b_qT[hp], b_kT[hp], b_vN[hp]
            att16 = [a16v[hp]]
            b_a16 = [b_a16v[hp]]
            if h + 1 < H:
                fetch_head_w(h + 1)
                nxt = qkv_units(h + 1)
            else:
                nxt = []
            n_nxt0 = len(nxt)
            tidx = 0
            for c in range(NCH):
                pend = [None]
                for il in range(TPC):
                    ig = c * TPC + il
                    W = (ig + 1) * 128
                    par = ig % 2
                    a16s, ac, dg = att16[0], acol[par], diag16[par]
                    ba16, bac, bdg = b_a16[0], b_acol[par], b_diag[par]
                    HSC = min(T, 2048) // 2
                    if W <= HSC:
                        base = HSC * (ig % 2)
                        bpsc = [b_psc[ig % 2]]
                    else:
                        base = 0
                        bpsc = b_psc
                    chunks = [(tkc, min(512, W - tkc))
                              for tkc in range(0, W, 512)]
                    nch = len(chunks)
                    for ci, (tkc, wdt) in enumerate(chunks):
                        hb = ([b_psc[(base + tkc) // HSC]]
                              if wdt + tkc + base <= HSC or base + tkc >= HSC
                              else bpsc)

                        def em_scc(e, ig=ig, tkc=tkc, wdt=wdt, base=base,
                                   qT=qT, kT=kT):
                            diag_in = tkc <= ig * 128 < tkc + wdt
                            last = e.matmul(
                                p_sc[:, base + tkc:base + tkc + wdt],
                                qT[:, ig * 128:(ig + 1) * 128],
                                kT[:, tkc:tkc + wdt],
                                start=True, stop=not diag_in)
                            if diag_in:
                                last = e.matmul(
                                    p_sc[:, base + ig * 128:
                                         base + (ig + 1) * 128],
                                    ident16[:], tri16[:],
                                    start=False, stop=True)
                            return last
                        P.add("tensor", em_scc,
                              reads=[bqT, bkT, b_ident, b_tri],
                              writes=hb, tag=f"h{h}.sc{ig}.{ci}")
                        if nch == 1:
                            P.add("vector",
                                  lambda e, W=W, ac=ac, base=base:
                                  e.reduce_max(ac[:, 0:1],
                                               p_sc[:, base:base + W],
                                               axis=AX, negate=True),
                                  reads=hb, writes=[bac],
                                  tag=f"h{h}.mx{ig}")
                        else:
                            P.add("vector",
                                  lambda e, ci=ci, tkc=tkc, wdt=wdt,
                                  ac=ac, base=base:
                                  e.reduce_max(ac[:, 4 + ci:5 + ci],
                                               p_sc[:, base + tkc:
                                                    base + tkc + wdt],
                                               axis=AX),
                                  reads=hb, writes=[bac],
                                  tag=f"h{h}.mxc{ig}.{ci}")
                        if ci == 0 and pend[0] is not None:
                            pend[0]()
                            pend[0] = None
                            tidx += 1
                            if tidx >= 2:
                                share = (n_nxt0 * (tidx - 1)) // \
                                    max(1, NT - 2)
                                while nxt and n_nxt0 - len(nxt) < share:
                                    nxt.pop(0)()
                    if nch > 1:
                        P.add("vector",
                              lambda e, ac=ac, nch=nch: e.reduce_max(
                                  ac[:, 0:1], ac[:, 4:4 + nch], axis=AX,
                                  negate=True),
                              reads=[bac], writes=[bac],
                              tag=f"h{h}.mx{ig}")
                    if W <= HSC:
                        P.add("scalar",
                              lambda e, W=W, a16s=a16s, ac=ac, base=base:
                              e.activation(
                                  a16s[:, 0:W], p_sc[:, base:base + W],
                                  AF.Exp, bias=ac[:, 0:1],
                                  accum_out=ac[:, 1:2]),
                              reads=bpsc + [bac], writes=[ba16, bac],
                              tag=f"h{h}.ex{ig}")
                        P.add("vector",
                              lambda e, ac=ac: e.reciprocal(ac[:, 2:3],
                                                            ac[:, 1:2]),
                              reads=[bac], writes=[bac],
                              tag=f"h{h}.rc{ig}")
                    else:
                        P.add("scalar",
                              lambda e, a16s=a16s, ac=ac, HSC=HSC:
                              e.activation(
                                  a16s[:, 0:HSC], p_sc[:, 0:HSC],
                                  AF.Exp, bias=ac[:, 0:1],
                                  accum_out=ac[:, 1:2]),
                              reads=[b_psc[0], bac], writes=[ba16, bac],
                              tag=f"h{h}.exlo{ig}")
                        P.add("scalar",
                              lambda e, W=W, a16s=a16s, ac=ac, HSC=HSC:
                              e.activation(
                                  a16s[:, HSC:W], p_sc[:, HSC:W],
                                  AF.Exp, bias=ac[:, 0:1],
                                  accum_out=ac[:, 3:4]),
                              reads=[b_psc[1], bac], writes=[ba16, bac],
                              tag=f"h{h}.exhi{ig}")
                        P.add("vector",
                              lambda e, ac=ac: e.scalar_tensor_tensor(
                                  ac[:, 1:2], ac[:, 1:2], 1.0, ac[:, 3:4],
                                  ALU.mult, ALU.add),
                              reads=[bac], writes=[bac],
                              tag=f"h{h}.zs{ig}")
                        P.add("vector",
                              lambda e, ac=ac: e.reciprocal(ac[:, 2:3],
                                                            ac[:, 1:2]),
                              reads=[bac], writes=[bac],
                              tag=f"h{h}.rc{ig}")
                    P.add("vector",
                          lambda e, ac=ac, dg=dg: e.tensor_scalar(
                              dg[:], ident16[:], ac[:, 2:3], 0.0,
                              ALU.mult, ALU.add),
                          reads=[bac, b_ident], writes=[bdg],
                          tag=f"h{h}.dg{ig}")

                    def make_flush(ig=ig, il=il, c=c, a16s=a16s, dg=dg,
                                   ba16=ba16, bdg=bdg):
                        def flush():
                            ntile = ig + 1
                            for bi in range((ntile + 3) // 4):
                                k0, k1 = bi * 4, min(ntile, bi * 4 + 4)
                                half = bi % 2
                                bptr = b_ptr[half]

                                def em_atr(e, a16s=a16s, dg=dg, k0=k0,
                                           k1=k1, half=half):
                                    last = None
                                    for kt in range(k0, k1):
                                        last = e.matmul(
                                            p_tr[:, half * 512 +
                                                 (kt - k0) * 128:
                                                 half * 512 +
                                                 (kt - k0 + 1) * 128],
                                            a16s[:, kt * 128:(kt + 1) * 128],
                                            dg[:], start=True, stop=True)
                                    return last
                                P.add("tensor", em_atr,
                                      reads=[ba16, bdg], writes=[bptr],
                                      tag=f"h{h}.atr{ig}.{bi}")

                                def em_acp(e, k0=k0, k1=k1, half=half,
                                           il=il):
                                    srcp = p_tr[:, half * 512:
                                                half * 512 +
                                                (k1 - k0) * 128]
                                    dst = attTb3()[:, k0:k1,
                                                   il * 128:(il + 1) * 128]
                                    return e.copy(dst, srcp)
                                P.add("scalar", em_acp, reads=[bptr],
                                      writes=[b_attTb],
                                      tag=f"h{h}.acp{ig}.{bi}")
                            ntail = (c + 1) * TPC - ntile
                            if ntail > 0:
                                def em_z(e, ntile=ntile, ntail=ntail,
                                         il=il):
                                    dst = attTb3()[:, ntile:ntile + ntail,
                                                   il * 128:(il + 1) * 128]
                                    return e.memset(dst, 0.0)
                                P.add("gpsimd", em_z, writes=[b_attTb],
                                      tag=f"h{h}.zz{ig}")
                        return flush
                    pend[0] = make_flush()
                if pend[0] is not None:
                    pend[0]()
                    pend[0] = None

                nkt = (c + 1) * TPC

                def em_av(e, c=c, nkt=nkt, vN=vN):
                    last = None
                    for kt in range(nkt):
                        last = e.matmul(p_av[:, 0:CH],
                                        vN[:, kt * 128:(kt + 1) * 128],
                                        attTb3()[:, kt, 0:CH],
                                        start=(kt == 0), stop=(kt == nkt - 1))
                    return last
                P.add("tensor", em_av, reads=[bvN, b_attTb], writes=[b_pav],
                      tag=f"h{h}.av{c}")

                def em_oev(e, c=c, h=h):
                    return e.activation(oT[:, c * CH:(c + 1) * CH],
                                        p_av[:, 0:CH], AF.Identity,
                                        bias=bv_col[h % 2][:], scale=1.0)
                P.add("scalar", em_oev, reads=[b_pav, b_bvc[h % 2]],
                      writes=[b_oT], tag=f"h{h}.oev{c}")

            for u in nxt:
                u()
            dma_step("sync", [(d["attnT"][h], oT[:])], reads=[b_oT],
                     tag=f"h{h}.osp")

        P.barrier()

    with contextlib.ExitStack() as es:
        at_sb = [es.enter_context(
            nc.sbuf_tensor(f"at_sb{i}", [128, KD * 128], BF16))
            for i in range(3)]
        b_at = [Buf(f"at{i}") for i in range(3)]
        xo_sb = [es.enter_context(
            nc.sbuf_tensor(f"xo_sb{i}", [128, JH], F32)) for i in range(2)]
        b_xo = [Buf("xo0"), Buf("xo1")]
        x1_sb = [es.enter_context(
            nc.sbuf_tensor(f"x1_sb{i}", [128, JH], F32)) for i in range(2)]
        b_x1 = [Buf("x1o0"), Buf("x1o1")]
        bo16 = es.enter_context(nc.sbuf_tensor("bo16", [1, D], BF16))
        b_bo = Buf("bo16")
        p_o = [es.enter_context(
            nc.psum_tensor(f"p_o{i}", [128, min(JH, 2048)], F32))
            for i in range(2)]
        b_po = [Buf("po0"), Buf("po1")]

        dma_step("gpsimd", [(bo16[:], d["bo"])], writes=[b_bo], tag="bo")

        OPROJ_PING = 2 * JH <= LNW
        if OPROJ_PING:
            b_woq = [[Buf(f"woq{q}_{i}") for i in range(4)]
                     for q in range(2)]
            QOF = JH
        else:
            _bw = [Buf("woq_0")] * 4
            b_woq = [_bw, _bw]
            QOF = 0

        for jh in range(NJH):
            qp = jh % 2
            ksp = max(1, KD // 4)
            for si in range(0, KD, ksp):
                pairs = [(lnT3()[:, k, qp * QOF:qp * QOF + JH],
                          d["wo"][k, :, jh * JH:(jh + 1) * JH])
                         for k in range(si, min(KD, si + ksp))]
                dma_step("gpsimd", pairs, writes=[b_woq[qp][si // ksp]],
                         tag=f"op.w{jh}.{si}")
            for tc in range(NT):
                par = tc % 3
                a_s, b_as = at_sb[par], b_at[par]
                x_s, b_xs = xo_sb[tc % 2], b_xo[tc % 2]
                x1s, b_x1s = x1_sb[tc % 2], b_x1[tc % 2]
                po, b_pos = p_o[tc % 2], b_po[tc % 2]
                npart = (KD + 7) // 8
                pairs = []
                a3 = a_s[:].rearrange("p (t j) -> p t j", t=KD)
                for gi in range(npart):
                    k0, k1 = gi * 8, min(KD, gi * 8 + 8)
                    src = d["attnT"][k0:k1, :, tc * 128:(tc + 1) * 128]
                    pairs.append((a3[:, k0:k1, :], src.transpose([1, 0, 2])))
                dma_step("sync", pairs, writes=[b_as], tag=f"op.at{jh}.{tc}")
                dma_step("scalar",
                         [(x_s[:], d["x"][tc * 128:(tc + 1) * 128,
                                          jh * JH:(jh + 1) * JH])],
                         writes=[b_xs], tag=f"op.x{jh}.{tc}")

                def em_omm(e, a_s=a_s, po=po, jh=jh, qp=qp):
                    last = None
                    for k in range(KD):
                        for jc in range(0, JH, 512):
                            wdt = min(512, JH - jc)
                            last = e.matmul(
                                po[:, jc:jc + wdt],
                                a_s[:, k * 128:(k + 1) * 128],
                                lnT3()[:, k, qp * QOF + jc:
                                       qp * QOF + jc + wdt],
                                start=(k == 0), stop=False)
                    for jc in range(0, JH, 512):
                        wdt = min(512, JH - jc)
                        last = e.matmul(
                            po[:, jc:jc + wdt], ones16[:, 0:128],
                            bo16[:, jh * JH + jc:jh * JH + jc + wdt],
                            start=False, stop=True)
                    return last
                P.add("tensor", em_omm,
                      reads=[b_as, b_ones, b_bo] + list(b_woq[qp]),
                      writes=[b_pos], tag=f"op.mm{jh}.{tc}")

                def em_oadd(e, x1s=x1s, po=po, x_s=x_s):
                    return e.tensor_tensor(x1s[:], po[:, 0:JH], x_s[:],
                                           ALU.add)
                P.add("vector", em_oadd, reads=[b_pos, b_xs], writes=[b_x1s],
                      tag=f"op.add{jh}.{tc}")
                dma_step("sync",
                         [(d["x1"][tc * 128:(tc + 1) * 128,
                                   jh * JH:(jh + 1) * JH], x1s[:])],
                         reads=[b_x1s], tag=f"op.st{jh}.{tc}")
        P.barrier()

    emit_layernorm(d["x1"], 2)
    P.barrier()

    with contextlib.ExitStack() as es:
        hot32 = es.enter_context(
            nc.sbuf_tensor("hot32", [KD, KD * 128], BF16))
        b_h32 = Buf("hot32")
        conehot = es.enter_context(
            nc.sbuf_tensor("conehot", [128, KD * KD], BF16))
        b_coh = Buf("conehot")
        dma_step("sync", [(hot32[:], d["chot32"]),
                          (conehot[:], d["cconehot"])],
                 writes=[b_h32, b_coh], tag="hots")
        xgT = es.enter_context(nc.sbuf_tensor("xgT", [KD, T], BF16))
        b_xgT = Buf("xgT")
        wstb = [es.enter_context(
            nc.sbuf_tensor(f"wst{i}", [128, KD * SJ], BF16))
            for i in range(2)]
        b_wst = [Buf("wst0"), Buf("wst1")]
        KH = KD // 2 if KD % 2 == 0 else KD
        nibgu = es.enter_context(
            nc.sbuf_tensor("nibgu", [128, KH * SJ], I32))
        b_nib = Buf("nibgu")
        pk_sb, b_pk = {}, {}
        ssch, nzs, znib, qzst, b_ss, b_nzs, b_zs, b_qz = ({}, {}, {}, {},
                                                          {}, {}, {}, {})
        for m in ("g", "u"):
            pk_sb[m] = [es.enter_context(
                nc.sbuf_tensor(f"pk{m}{i}", [128, KD * PC], I32))
                for i in range(2)]
            ssch[m] = [es.enter_context(
                nc.sbuf_tensor(f"ssch{m}{i}", [KD, SJ], BF16))
                for i in range(2)]
            nzs[m] = [es.enter_context(
                nc.sbuf_tensor(f"nzs{m}{i}", [KD, SJ], BF16))
                for i in range(2)]
            znib[m] = [es.enter_context(
                nc.sbuf_tensor(f"znib{m}{i}", [KD, SJ], I32))
                for i in range(2)]
            qzst[m] = [es.enter_context(
                nc.sbuf_tensor(f"qzst{m}{i}", [KD, PC], I32))
                for i in range(2)]
            b_pk[m] = [Buf(f"pk{m}{i}") for i in range(2)]
            b_ss[m] = [Buf(f"ss{m}{i}") for i in range(2)]
            b_nzs[m] = [Buf(f"nzs{m}{i}") for i in range(2)]
            b_zs[m] = [Buf(f"zs{m}{i}") for i in range(2)]
            b_qz[m] = [Buf(f"qz{m}{i}") for i in range(2)]
        silu_sb = es.enter_context(nc.sbuf_tensor("silu_sb", [128, T], BF16))
        b_silu = Buf("silu")
        hTb = [es.enter_context(
            nc.sbuf_tensor(f"hTb{i}", [128, T], BF16)) for i in range(2)]
        b_hTb = [Buf("hTb0"), Buf("hTb1")]
        p_gu = {m: es.enter_context(
            nc.psum_tensor(f"p_{m}", [128, 1024], F32)) for m in ("g", "u")}
        b_pgu = {m: Buf(f"p_{m}") for m in ("g", "u")}
        p_s = [es.enter_context(
            nc.psum_tensor(f"p_s{i}", [128, 512], F32)) for i in range(4)]
        b_ps = [Buf(f"ps{i}") for i in range(4)]

        state = {"bc": 0}

        NT5X = (T + 511) // 512
        for tch in range(NT5X):
            w5 = min(512, T - tch * 512)

            def em_xg(e, tch=tch, w5=w5):
                last = None
                for g in range(KD):
                    last = e.matmul(
                        p_s[tch % 4][0:KD, 0:w5],
                        conehot[:, g * KD:(g + 1) * KD],
                        lnT3()[:, g, tch * 512:tch * 512 + w5],
                        start=(g == 0), stop=(g == KD - 1))
                return last
            P.add("tensor", em_xg, reads=[b_coh, b_lnT],
                  writes=[b_ps[tch % 4]], tag=f"xg.mm{tch}")
            P.add("vector",
                  lambda e, tch=tch, w5=w5: e.tensor_copy(
                      xgT[0:KD, tch * 512:tch * 512 + w5],
                      p_s[tch % 4][0:KD, 0:w5]),
                  reads=[b_ps[tch % 4]], writes=[b_xgT], tag=f"xg.cp{tch}")

        NKH = (KD + KH - 1) // KH
        n_main = NST * 2

        def deq_stage_parts(sidx):
            s = sidx // 2
            m = "g" if sidx % 2 == 0 else "u"
            par = sidx % 2
            sp = s % 2
            dma_step("sync", [(pk_sb[m][sp][:], d[m + "qw"][s]),
                              (qzst[m][sp][:], d[m + "qz"][s])],
                     writes=[b_pk[m][sp], b_qz[m][sp]], tag=f"{m}{s}.pk")
            dma_step("gpsimd", [(ssch[m][sp][:], d[m + "sc"][s])],
                     writes=[b_ss[m][sp]], tag=f"{m}{s}.sc")

            def zero_part():
                def em_uz(e, m=m, sp=sp):
                    last = None
                    for i in range(8):
                        last = e.tensor_scalar(
                            znib[m][sp][0:KD, PI[i]:SJ:8], qzst[m][sp][:],
                            4 * i, 15, ALU.logical_shift_right,
                            ALU.bitwise_and)
                    return last
                P.add("vector", em_uz, reads=[b_qz[m][sp]],
                      writes=[b_zs[m][sp]], tag=f"{m}{s}.upz")
                P.add("vector",
                      lambda e, m=m, sp=sp: e.scalar_tensor_tensor(
                          nzs[m][sp][:], znib[m][sp][0:KD, :], -1.0,
                          ssch[m][sp][:], ALU.mult, ALU.mult),
                      reads=[b_zs[m][sp], b_ss[m][sp]],
                      writes=[b_nzs[m][sp]], tag=f"{m}{s}.nzs")

            GB = max(1, 512 // SJ)

            def half_part(kh):
                h0 = kh * KH
                h1 = min(KD, h0 + KH)

                def em_uw(e, m=m, h0=h0, h1=h1, sp=sp):
                    last = None
                    w3 = nibgu[:].rearrange("p (g j) -> p g j", g=KH)
                    for i in range(8):
                        last = e.tensor_scalar(
                            w3[:, 0:h1 - h0, PI[i]:SJ:8],
                            pk_sb[m][sp][:, h0 * PC:h1 * PC],
                            4 * i, 15, ALU.logical_shift_right,
                            ALU.bitwise_and)
                    return last
                P.add("vector", em_uw, reads=[b_pk[m][sp]],
                      writes=[b_nib], tag=f"{m}{s}.upw{kh}")

                for b0 in range(h0, h1, GB):
                    b1 = min(h1, b0 + GB)
                    bp = state["bc"]
                    state["bc"] = (bp + 1) % 4

                    def em_bc(e, b0=b0, b1=b1, m=m, bp=bp, sp=sp):
                        last = None
                        for g in range(b0, b1):
                            last = e.matmul(
                                p_s[bp][:, (g - b0) * SJ:(g - b0 + 1) * SJ],
                                hot32[0:KD, g * 128:(g + 1) * 128],
                                ssch[m][sp][0:KD, :], start=True, stop=True)
                        return last
                    P.add("tensor", em_bc,
                          reads=[b_h32, b_ss[m][sp]],
                          writes=[b_ps[bp]], tag=f"{m}{s}.bc{b0}")
                    nsl = nibgu[:, (b0 - h0) * SJ:(b1 - h0) * SJ]
                    sl = wstb[par][:, b0 * SJ:b1 * SJ]
                    P.add("vector",
                          lambda e, sl=sl, nsl=nsl, b0=b0, b1=b1, bp=bp:
                          e.tensor_tensor(
                              sl, nsl, p_s[bp][:, 0:(b1 - b0) * SJ],
                              ALU.mult),
                          reads=[b_ps[bp], b_nib, b_wst[par]],
                          writes=[b_wst[par]], tag=f"{m}{s}.mul{b0}")

            parts = [lambda: (zero_part(), half_part(0))]
            for kh in range(1, NKH):
                parts.append(lambda kh=kh: half_part(kh))
            return parts

        def mm_pr(sidx, pr):
            s = sidx // 2
            m = "g" if sidx % 2 == 0 else "u"
            par = sidx % 2
            sp = s % 2
            prw = min(1024, T - pr)

            def em_mm(e, m=m, par=par, pr=pr, prw=prw, sp=sp):
                last = None
                for k in range(KD):
                    for tci in range(0, prw, 512):
                        wdt = min(512, prw - tci)
                        last = e.matmul(
                            p_gu[m][:, tci:tci + wdt],
                            wstb[par][:, k * SJ:(k + 1) * SJ],
                            lnT3()[:, k, pr + tci:pr + tci + wdt],
                            start=(k == 0), stop=False)
                for tci in range(0, prw, 512):
                    wdt = min(512, prw - tci)
                    last = e.matmul(
                        p_gu[m][:, tci:tci + wdt], nzs[m][sp][:],
                        xgT[0:KD, pr + tci:pr + tci + wdt],
                        start=False, stop=True)
                return last
            P.add("tensor", em_mm,
                  reads=[b_wst[par], b_lnT, b_nzs[m][sp], b_xgT],
                  writes=[b_pgu[m]], tag=f"{m}{s}.mm{pr}")
            if m == "g":
                P.add("scalar",
                      lambda e, pr=pr, prw=prw: e.activation(
                          silu_sb[:, pr:pr + prw], p_gu["g"][:, 0:prw],
                          AF.Silu),
                      reads=[b_pgu["g"]], writes=[b_silu],
                      tag=f"{m}{s}.si{pr}")
            else:
                hp = s % 2
                P.add("vector",
                      lambda e, pr=pr, prw=prw, hp=hp:
                      e.scalar_tensor_tensor(
                          hTb[hp][:, pr:pr + prw], p_gu["u"][:, 0:prw],
                          1.0, silu_sb[:, pr:pr + prw],
                          ALU.mult, ALU.mult),
                      reads=[b_pgu["u"], b_silu], writes=[b_hTb[hp]],
                      tag=f"{m}{s}.hm{pr}")

        INTERLEAVE_DEQ = True
        prs = list(range(0, T, 1024))
        pending = deq_stage_parts(0)
        for p in pending:
            p()
        for sidx in range(n_main):
            s = sidx // 2
            m = "g" if sidx % 2 == 0 else "u"
            nxt = deq_stage_parts(sidx + 1) if sidx + 1 < n_main else []
            for pi, pr in enumerate(prs):
                mm_pr(sidx, pr)
                if INTERLEAVE_DEQ:
                    share = (len(nxt) * (pi + 1)) // max(1, len(prs))
                    while len(nxt) and share > 0:
                        nxt.pop(0)()
                        share -= 1
            for p in nxt:
                p()
            if m == "u":
                hp = s % 2
                dma_step("sync", [(d["hT"][s], hTb[hp][:])],
                         reads=[b_hTb[hp]], tag=f"hT.st{s}")
        P.barrier()

    outer_es.close()

    with contextlib.ExitStack() as es:
        wdn_sb = es.enter_context(
            nc.sbuf_tensor("wdn_sb", [128, JG * DNW], BF16))
        b_wg = [Buf(f"wdn{g}") for g in range(JG)]
        hT_sb = [es.enter_context(
            nc.sbuf_tensor(f"hT_sb{i}", [128, TPR * 128], BF16))
            for i in range(6)]
        b_hts = [Buf(f"hts{i}") for i in range(6)]
        x1t = [es.enter_context(
            nc.sbuf_tensor(f"x1t{i}", [128, DNW], F32)) for i in range(4)]
        b_x1t = [Buf(f"x1t{i}") for i in range(4)]
        ot_sb = [es.enter_context(
            nc.sbuf_tensor(f"ot_sb{i}", [128, DNW], F32)) for i in range(4)]
        b_ot = [Buf(f"ot{i}") for i in range(4)]
        DGB = 4
        dpk = [es.enter_context(
            nc.sbuf_tensor(f"dpk{i}", [128, DGB * (DNW // 8)], I32))
            for i in range(2)]
        dnib = [es.enter_context(
            nc.sbuf_tensor(f"dnib{i}", [128, DGB * DNW], I32))
            for i in range(2)]
        dssc = [es.enter_context(
            nc.sbuf_tensor(f"dssc{i}", [JG, DNW], BF16)) for i in range(2)]
        dzsc = [es.enter_context(
            nc.sbuf_tensor(f"dzsc{i}", [JG, DNW], BF16)) for i in range(2)]
        dznb = [es.enter_context(
            nc.sbuf_tensor(f"dznb{i}", [JG, DNW], I32)) for i in range(2)]
        dqzs = [es.enter_context(
            nc.sbuf_tensor(f"dqzs{i}", [JG, DNW // 8], I32))
            for i in range(2)]
        hotsl = [es.enter_context(
            nc.sbuf_tensor(f"hotsl{i}", [JG, 128], BF16)) for i in range(4)]
        b_hsl = [Buf(f"hsl{i}") for i in range(4)]
        b_dpk = [Buf("dpk0"), Buf("dpk1")]
        b_dnib = [Buf("dnib0"), Buf("dnib1")]
        b_dss = [Buf("dss0"), Buf("dss1")]
        b_dzs = [Buf("dzs0"), Buf("dzs1")]
        b_dznb = [Buf("dznb0"), Buf("dznb1")]
        b_dqz = [Buf("dqz0"), Buf("dqz1")]
        p_dn = [es.enter_context(
            nc.psum_tensor(f"p_dn{i}", [128, 512], F32)) for i in range(TPR)]
        b_pdn = [Buf(f"pdn{i}") for i in range(TPR)]
        p_zs = [es.enter_context(
            nc.psum_tensor(f"p_zs{i}", [128, 512], F32)) for i in range(4)]
        b_pzs = [Buf(f"pzs{i}") for i in range(4)]

        def emit_down_unit(nci, g):
            np_ = nci % 2
            if g == 0:
                dma_step("gpsimd", [(dssc[np_][:], d["dsc"][nci])],
                         writes=[b_dss[np_]], tag=f"dn.sc{nci}")
                dma_step("scalar", [(dqzs[np_][:], d["dqz"][nci])],
                         writes=[b_dqz[np_]], tag=f"dn.qz{nci}")

                def em_uz(e, np_=np_):
                    last = None
                    for i in range(8):
                        last = e.tensor_scalar(
                            dznb[np_][0:JG, PI[i]:DNW:8], dqzs[np_][:],
                            4 * i, 15, ALU.logical_shift_right,
                            ALU.bitwise_and)
                    return last
                P.add("vector", em_uz, reads=[b_dqz[np_]],
                      writes=[b_dznb[np_]], tag=f"dn.uz{nci}")
                P.add("vector",
                      lambda e, np_=np_: e.tensor_copy(
                          dzsc[np_][:], dznb[np_][0:JG, :]),
                      reads=[b_dznb[np_]], writes=[b_dzs[np_]],
                      tag=f"dn.cz{nci}")
            dp = (g // DGB) % 2
            if g % DGB == 0:
                g1 = min(JG, g + DGB)
                ng = g1 - g
                pcw = DNW // 8
                dma_step("scalar",
                         [(dpk[dp][:, 0:ng * pcw],
                           d["dqw"][nci, :, g * pcw:g1 * pcw])],
                         writes=[b_dpk[dp]], tag=f"dn.pk{nci}.{g}")

                def em_uw(e, ng=ng, pcw=pcw, dp=dp):
                    last = None
                    w3 = dnib[dp][:].rearrange("p (g w) -> p g w", g=DGB)
                    for i in range(8):
                        last = e.tensor_scalar(
                            w3[:, 0:ng, PI[i]:DNW:8],
                            dpk[dp][:, 0:ng * pcw],
                            4 * i, 15, ALU.logical_shift_right,
                            ALU.bitwise_and)
                    return last
                P.add("vector", em_uw, reads=[b_dpk[dp]],
                      writes=[b_dnib[dp]], tag=f"dn.uw{nci}.{g}")
            hp = g % 4
            dma_step("scalar",
                     [(hotsl[hp][:],
                       d["chotJG"][:, g * 128:(g + 1) * 128])],
                     writes=[b_hsl[hp]], tag=f"dn.hs{nci}.{g}")
            zp = 2 * (g % 2)

            def em_bc(e, hp=hp, zp=zp, np_=np_):
                e.matmul(p_zs[zp][:, 0:DNW], hotsl[hp][:],
                         dzsc[np_][0:JG, 0:DNW], start=True, stop=True)
                return e.matmul(p_zs[zp + 1][:, 0:DNW], hotsl[hp][:],
                                dssc[np_][0:JG, 0:DNW],
                                start=True, stop=True)
            P.add("tensor", em_bc,
                  reads=[b_hsl[hp], b_dzs[np_], b_dss[np_]],
                  writes=[b_pzs[zp], b_pzs[zp + 1]], tag=f"dn.bc{nci}.{g}")
            nsl = dnib[dp][:, (g % DGB) * DNW:(g % DGB + 1) * DNW]
            sl = wdn_sb[:, g * DNW:(g + 1) * DNW]
            P.add("vector",
                  lambda e, sl=sl, nsl=nsl, zp=zp: e.tensor_tensor(
                      sl, nsl, p_zs[zp][:, 0:DNW], ALU.subtract),
                  reads=[b_pzs[zp], b_dnib[dp]], writes=[b_wg[g]],
                  tag=f"dn.sub{nci}.{g}")
            P.add("vector",
                  lambda e, sl=sl, zp=zp: e.tensor_tensor(
                      sl, sl, p_zs[zp + 1][:, 0:DNW], ALU.mult),
                  reads=[b_pzs[zp + 1]], writes=[b_wg[g]],
                  tag=f"dn.mul{nci}.{g}")

        for g in range(JG):
            emit_down_unit(0, g)
        for nci in range(NC_DN):
            for tr in range(NTR):
                for g in range(JG):
                    par = g % 6
                    hs, bhs = hT_sb[par], b_hts[par]
                    dma_step("sync",
                             [(hs[:], d["hT"][g][:, tr * TPR * 128:
                                                 (tr + 1) * TPR * 128])],
                             writes=[bhs], tag=f"dm.h{nci}.{tr}.{g}")

                    def em_mm(e, g=g, hs=hs):
                        last = None
                        for ti in range(TPR):
                            last = e.matmul(
                                p_dn[ti][:, 0:DNW],
                                hs[:, ti * 128:(ti + 1) * 128],
                                wdn_sb[:, g * DNW:(g + 1) * DNW],
                                start=(g == 0), stop=(g == JG - 1))
                        return last
                    P.add("tensor", em_mm, reads=[bhs, b_wg[g]],
                          writes=b_pdn, tag=f"dm.mm{nci}.{tr}.{g}")
                    if tr == NTR - 1 and nci + 1 < NC_DN:
                        emit_down_unit(nci + 1, g)
                for ti in range(TPR):
                    tc = tr * TPR + ti
                    qpar = ti % 4
                    xs, bxs = x1t[qpar], b_x1t[qpar]
                    os_, bos = ot_sb[qpar], b_ot[qpar]
                    dma_step("scalar",
                             [(xs[:], d["x1"][tc * 128:(tc + 1) * 128,
                                              nci * DNW:(nci + 1) * DNW])],
                             writes=[bxs], tag=f"dm.x{nci}.{tc}")
                    P.add("vector",
                          lambda e, ti=ti, os_=os_, xs=xs: e.tensor_tensor(
                              os_[:], p_dn[ti][:, 0:DNW], xs[:], ALU.add),
                          reads=[b_pdn[ti], bxs], writes=[bos],
                          tag=f"dm.add{nci}.{tc}")
                    dma_step("sync",
                             [(d["out"][tc * 128:(tc + 1) * 128,
                                        nci * DNW:(nci + 1) * DNW], os_[:])],
                             reads=[bos], tag=f"dm.st{nci}.{tc}")


    finals = [(DMA_SEM[q] + str(r), 16 * P.dma_count[q][r])
              for q in DMA_SEM for r in range(NLANE)
              if P.dma_count[q][r] > 0]

    n_instr = P.emit_all(nc, sems)

    with nc.Block() as block:
        @block.sync
        def _(sync):
            for name, val in finals:
                sync.wait_ge(sems[name], val)

    nc.finalize()
    return nc, n_instr



def prep_common(cfg, Wq, bq, Wk, bk, Wv, bv, Wo, bo,
                gqw, gsc, gqz, uqw, usc, uqz, dqw, dsc, dqz):
    T, D, H, I = cfg["T"], cfg["D"], cfg["H"], cfg["I"]
    KD, NST, PC, SJ = cfg["KD"], cfg["NST"], cfg["PC"], cfg["SJ"]
    JG, NC_DN, DNW = cfg["JG"], cfg["NC_DN"], cfg["DNW"]

    def headify(W):
        WT = np.ascontiguousarray(W.T)
        r = WT.reshape(KD, 128, H, 128)
        return np.ascontiguousarray(
            r.transpose(2, 1, 0, 3).reshape(H, 128, KD * 128))

    def pack_qw(qw, nst, pc, ktiles):
        r = qw.reshape(ktiles, 128, nst, pc)
        return np.ascontiguousarray(
            r.transpose(2, 1, 0, 3).reshape(nst, 128, ktiles * pc))

    def pack_rows(a, nst, w, groups):
        r = a.reshape(groups, nst, w)
        return np.ascontiguousarray(r.transpose(1, 0, 2))

    common = {
        "wq": headify(Wq), "wk": headify(Wk), "wv": headify(Wv),
        "wo": np.ascontiguousarray(Wo.T).reshape(KD, 128, D),
        "bq": np.ascontiguousarray(bq.reshape(1, D)),
        "bk": np.ascontiguousarray(bk.reshape(1, D)),
        "bvc": np.ascontiguousarray(bv.reshape(D, 1)),
        "bo": np.ascontiguousarray(bo.reshape(1, D)),
        "gqw": pack_qw(gqw, NST, PC, KD),
        "gsc": pack_rows(gsc, NST, SJ, KD),
        "gqz": pack_rows(gqz, NST, PC, KD),
        "uqw": pack_qw(uqw, NST, PC, KD),
        "usc": pack_rows(usc, NST, SJ, KD),
        "uqz": pack_rows(uqz, NST, PC, KD),
        "dqw": pack_qw(dqw, NC_DN, DNW // 8, JG),
        "dsc": pack_rows(dsc, NC_DN, DNW, JG),
        "dqz": pack_rows(dqz, NC_DN, DNW // 8, JG),
    }
    common.update(host_consts(cfg))
    return common


def kernel(L0_attn_x, L0_attn_Wq, L0_attn_bq, L0_attn_Wk, L0_attn_bk,
           L0_attn_Wv, L0_attn_bv, L0_attn_Wo, L0_attn_bo,
           L0_ffn_gate_qweight, L0_ffn_gate_scales, L0_ffn_gate_qzeros,
           L0_ffn_up_qweight, L0_ffn_up_scales, L0_ffn_up_qzeros,
           L0_ffn_down_qweight, L0_ffn_down_scales, L0_ffn_down_qzeros):
    x = np.asarray(L0_attn_x, dtype=np.float32)
    B, T, D = x.shape
    cfg = make_cfg(T=T, D=D, H=32, I=11008)
    common = prep_common(
        cfg,
        np.asarray(L0_attn_Wq, np.float32), np.asarray(L0_attn_bq, np.float32),
        np.asarray(L0_attn_Wk, np.float32), np.asarray(L0_attn_bk, np.float32),
        np.asarray(L0_attn_Wv, np.float32), np.asarray(L0_attn_bv, np.float32),
        np.asarray(L0_attn_Wo, np.float32), np.asarray(L0_attn_bo, np.float32),
        np.asarray(L0_ffn_gate_qweight, np.int32),
        np.asarray(L0_ffn_gate_scales, np.float32),
        np.asarray(L0_ffn_gate_qzeros, np.int32),
        np.asarray(L0_ffn_up_qweight, np.int32),
        np.asarray(L0_ffn_up_scales, np.float32),
        np.asarray(L0_ffn_up_qzeros, np.int32),
        np.asarray(L0_ffn_down_qweight, np.int32),
        np.asarray(L0_ffn_down_scales, np.float32),
        np.asarray(L0_ffn_down_qzeros, np.int32))
    in_maps = []
    for c in range(B):
        m = dict(common)
        m["x"] = np.ascontiguousarray(x[c])
        in_maps.append(m)
    nc, _ = build_program(cfg)
    res = run_bass_kernel_spmd(nc, in_maps, core_ids=list(range(B)))
    out = np.stack([np.asarray(r["out"]) for r in res.results], axis=0)
    return out.astype(np.float32)


# version 27
# speedup vs baseline: 1.0501x; 1.0501x over previous
import contextlib

import numpy as np
import ml_dtypes

import concourse.bass as bass
import concourse.mybir as mybir
import concourse.bass_utils as bass_utils
from concourse.bass_utils import run_bass_kernel_spmd



BF = ml_dtypes.bfloat16
F32 = mybir.dt.float32
BF16 = mybir.dt.bfloat16
I32 = mybir.dt.int32
AF = mybir.ActivationFunctionType
ALU = mybir.AluOpType
AX = mybir.AxisListType.X

NEG = -1.0e30
PI = [0, 2, 4, 6, 1, 3, 5, 7]


ENGINES = ("tensor", "vector", "scalar", "gpsimd", "sync")
DMA_SEM = {"sync": "dS", "gpsimd": "dG", "scalar": "dA"}
ENG_SEM = {"tensor": "sP", "vector": "sV", "scalar": "sA", "gpsimd": "sG",
           "sync": "sS"}
NLANE = 4


def sem_key(t):
    if t.dma:
        return DMA_SEM[t.eng] + str(t.lane)
    return ENG_SEM[t.eng]


class Tok:
    __slots__ = ("eng", "dma", "val", "lane")

    def __init__(self, eng, dma, val, lane=0):
        self.eng = eng
        self.dma = dma
        self.val = val
        self.lane = lane


class Buf:
    __slots__ = ("name", "writer", "readers")

    def __init__(self, name):
        self.name = name
        self.writer = None
        self.readers = []


class Step:
    __slots__ = ("eng", "emit", "deps", "dma", "n_dma", "tok", "tag")


class Prog:
    def __init__(self):
        self.steps = {e: [] for e in ENGINES}
        self.dma_count = {e: [0] * NLANE for e in DMA_SEM}
        self.dma_seq = {e: 0 for e in DMA_SEM}
        self.last_dma_tok = {e: [None] * NLANE for e in DMA_SEM}
        self.comp_count = {e: 0 for e in ENGINES}
        self.pending_barrier = None
        self._barrier_seen = set()

    def add(self, eng, emit, reads=(), writes=(), deps=(), dma=False,
            n_dma=1, tag=""):
        dep_toks = list(deps)
        for b in reads:
            if b.writer is not None:
                dep_toks.append(b.writer)
        for b in writes:
            if b.writer is not None:
                dep_toks.append(b.writer)
            dep_toks.extend(b.readers)
        if self.pending_barrier is not None:
            dep_toks.extend(self.pending_barrier)
        st = Step()
        st.eng, st.emit, st.tag = eng, emit, tag
        st.dma = dma
        st.n_dma = n_dma if dma else 0
        if dma:
            lane = self.dma_seq[eng] % NLANE
            self.dma_seq[eng] += 1
            prev = self.last_dma_tok[eng][lane]
            if prev is not None:
                dep_toks.append(prev)
            self.dma_count[eng][lane] += st.n_dma
            tok = Tok(eng, True, 16 * self.dma_count[eng][lane], lane)
            self.last_dma_tok[eng][lane] = tok
        else:
            self.comp_count[eng] += 1
            tok = Tok(eng, False, self.comp_count[eng])
        st.deps = dep_toks
        st.tok = tok
        self.steps[eng].append(st)
        for b in reads:
            b.readers.append(tok)
        for b in writes:
            b.writer = tok
            b.readers = []
        if self.pending_barrier is not None:
            self._barrier_seen.add(eng)
            if self._barrier_seen == set(ENGINES):
                self.pending_barrier = None
        return tok

    def barrier(self):
        toks = [self.steps[e][-1].tok for e in ENGINES if self.steps[e]]
        for q in DMA_SEM:
            for t in self.last_dma_tok[q]:
                if t is not None:
                    toks.append(t)
        self.pending_barrier = toks
        self._barrier_seen = set()

    def verify(self):
        pos = {e: 0 for e in ENGINES}
        sem = {}

        def ok(t):
            return sem.get(sem_key(t), 0) >= t.val

        total = sum(len(s) for s in self.steps.values())
        done = 0
        progress = True
        while progress:
            progress = False
            for e in ENGINES:
                while pos[e] < len(self.steps[e]):
                    st = self.steps[e][pos[e]]
                    if not all(ok(t) for t in st.deps):
                        break
                    key = sem_key(st.tok)
                    sem[key] = sem.get(key, 0) + (16 * st.n_dma if st.dma
                                                  else 1)
                    pos[e] += 1
                    done += 1
                    progress = True
        if done != total:
            stuck = []
            for e in ENGINES:
                if pos[e] < len(self.steps[e]):
                    st = self.steps[e][pos[e]]
                    bad = [(t.eng, t.dma, t.val) for t in st.deps
                           if not ok(t)]
                    stuck.append((e, pos[e], st.tag, bad[:3]))
            raise RuntimeError(f"schedule deadlock: {done}/{total}; "
                               f"stuck={stuck[:4]}")

    def emit_all(self, nc, sems):
        self.verify()
        n_instr = [0]

        def make_body(ename):
            def body(eng):
                watermark = {}
                for st in self.steps[ename]:
                    for t in st.deps:
                        key = sem_key(t)
                        if watermark.get(key, 0) >= t.val:
                            continue
                        eng.wait_ge(sems[key], t.val)
                        watermark[key] = t.val
                        n_instr[0] += 1
                    if st.dma:
                        st.emit(eng, sems[sem_key(st.tok)])
                        n_instr[0] += st.n_dma
                    else:
                        last = st.emit(eng)
                        assert last is not None, f"empty step {st.tag}"
                        last.then_inc(sems[ENG_SEM[ename]], 1)
                        n_instr[0] += 1
            return body

        with nc.Block() as block:
            block.tensor(make_body("tensor"))
            block.vector(make_body("vector"))
            block.scalar(make_body("scalar"))
            block.gpsimd(make_body("gpsimd"))
            block.sync(make_body("sync"))
        return n_instr[0]



def make_cfg(T=2048, D=4096, H=32, I=11008):
    HD = 128
    assert D == H * HD and T % 128 == 0 and I % 128 == 0
    cfg = dict(T=T, D=D, H=H, I=I, HD=HD)
    cfg["NT"] = T // 128
    cfg["KD"] = D // 128
    cfg["JG"] = I // 128
    cfg["CH"] = min(512, T)
    cfg["TPC"] = cfg["CH"] // 128
    cfg["NCH"] = T // cfg["CH"]
    cfg["TC5"] = min(512, T)
    cfg["NT5"] = T // cfg["TC5"]
    cfg["JH"] = min(2048, D)
    cfg["NJH"] = D // cfg["JH"]
    cfg["PC"] = 16
    cfg["SJ"] = 8 * cfg["PC"]
    cfg["NST"] = I // cfg["SJ"]
    cfg["DNW"] = min(512, D)
    cfg["NC_DN"] = D // cfg["DNW"]
    cfg["TPR"] = min(4, cfg["NT"])
    cfg["NTR"] = cfg["NT"] // cfg["TPR"]
    return cfg


def host_consts(cfg):
    tri = np.zeros((128, 128), dtype=BF)
    for p in range(128):
        tri[p, p + 1:] = BF(NEG)
    ident = np.eye(128, dtype=BF)
    ones = np.ones((1, 512), dtype=BF)
    KD, JG = cfg["KD"], cfg["JG"]
    hot32 = np.zeros((KD, KD * 128), dtype=BF)
    for g in range(KD):
        hot32[g, g * 128:(g + 1) * 128] = 1
    hotJG = np.zeros((JG, JG * 128), dtype=BF)
    for g in range(JG):
        hotJG[g, g * 128:(g + 1) * 128] = 1
    conehot = np.zeros((128, KD * KD), dtype=BF)
    for g in range(KD):
        conehot[:, g * KD + g] = 1
    return dict(cident=ident, cones=ones, ctri=tri, chot32=hot32,
                chotJG=hotJG, cconehot=conehot)



def build_program(cfg):
    T, D, H, I, HD = cfg["T"], cfg["D"], cfg["H"], cfg["I"], cfg["HD"]
    NT, KD, JG = cfg["NT"], cfg["KD"], cfg["JG"]
    CH, TPC, NCH = cfg["CH"], cfg["TPC"], cfg["NCH"]
    TC5, NT5 = cfg["TC5"], cfg["NT5"]
    JH, NJH = cfg["JH"], cfg["NJH"]
    PC, SJ, NST = cfg["PC"], cfg["SJ"], cfg["NST"]
    DNW, NC_DN, TPR, NTR = cfg["DNW"], cfg["NC_DN"], cfg["TPR"], cfg["NTR"]

    nc = bass.Bass("TRN2", target_bir_lowering=False)

    d = {}
    d["x"] = nc.dram_tensor("x", [T, D], F32, kind="ExternalInput").ap()
    for w in ("wq", "wk", "wv"):
        d[w] = nc.dram_tensor(w, [H, 128, KD * 128], F32,
                              kind="ExternalInput").ap()
    d["wo"] = nc.dram_tensor("wo", [KD, 128, D], F32,
                             kind="ExternalInput").ap()
    d["bq"] = nc.dram_tensor("bq", [1, D], F32, kind="ExternalInput").ap()
    d["bk"] = nc.dram_tensor("bk", [1, D], F32, kind="ExternalInput").ap()
    d["bvc"] = nc.dram_tensor("bvc", [D, 1], F32, kind="ExternalInput").ap()
    d["bo"] = nc.dram_tensor("bo", [1, D], F32, kind="ExternalInput").ap()
    for m in ("g", "u"):
        d[m + "qw"] = nc.dram_tensor(m + "qw", [NST, 128, KD * PC], I32,
                                     kind="ExternalInput").ap()
        d[m + "sc"] = nc.dram_tensor(m + "sc", [NST, KD, SJ], F32,
                                     kind="ExternalInput").ap()
        d[m + "qz"] = nc.dram_tensor(m + "qz", [NST, KD, PC], I32,
                                     kind="ExternalInput").ap()
    d["dqw"] = nc.dram_tensor("dqw", [NC_DN, 128, JG * (DNW // 8)], I32,
                              kind="ExternalInput").ap()
    d["dsc"] = nc.dram_tensor("dsc", [NC_DN, JG, DNW], F32,
                              kind="ExternalInput").ap()
    d["dqz"] = nc.dram_tensor("dqz", [NC_DN, JG, DNW // 8], I32,
                              kind="ExternalInput").ap()
    d["cident"] = nc.dram_tensor("cident", [128, 128], BF16,
                                 kind="ExternalInput").ap()
    d["cones"] = nc.dram_tensor("cones", [1, 512], BF16,
                                kind="ExternalInput").ap()
    d["ctri"] = nc.dram_tensor("ctri", [128, 128], BF16,
                               kind="ExternalInput").ap()
    d["chot32"] = nc.dram_tensor("chot32", [KD, KD * 128], BF16,
                                 kind="ExternalInput").ap()
    d["cconehot"] = nc.dram_tensor("cconehot", [128, KD * KD], BF16,
                                   kind="ExternalInput").ap()
    d["chotJG"] = nc.dram_tensor("chotJG", [JG, JG * 128], BF16,
                                 kind="ExternalInput").ap()
    d["attnT"] = nc.dram_tensor("attnT", [H, 128, T], BF16).ap()
    d["x1"] = nc.dram_tensor("x1", [T, D], F32).ap()
    d["hT"] = nc.dram_tensor("hT", [JG, 128, T], BF16).ap()
    d["out"] = nc.dram_tensor("out", [T, D], F32, kind="ExternalOutput").ap()

    sems = {}
    for name in ENG_SEM.values():
        sems[name] = nc.alloc_semaphore(name)
    for name in DMA_SEM.values():
        for r in range(NLANE):
            sems[name + str(r)] = nc.alloc_semaphore(name + str(r))

    P = Prog()

    ident16 = nc.alloc_sbuf_tensor("ident16", [128, 128], BF16)
    ones16 = nc.alloc_sbuf_tensor("ones16", [1, 512], BF16)
    tri16 = nc.alloc_sbuf_tensor("tri16", [128, 128], BF16)
    LNW = max(T, JH)
    outer_es = contextlib.ExitStack()
    lnT = outer_es.enter_context(
        nc.sbuf_tensor("lnT", [128, KD * LNW], BF16))

    b_ident = Buf("ident16"); b_ones = Buf("ones16"); b_tri = Buf("tri16")
    b_lnT = Buf("lnT")

    def lnT3():
        return lnT[:].rearrange("p (k w) -> p k w", k=KD)

    def dma_step(eng_name, pairs, reads=(), writes=(), deps=(), tag=""):
        def emit(eng, dsem):
            for dst, src in pairs:
                eng.dma_start(dst, src).then_inc(dsem, 16)
        return P.add(eng_name, emit, reads=reads, writes=writes, deps=deps,
                     dma=True, n_dma=len(pairs), tag=tag)

    dma_step("sync", [(ident16[:], d["cident"]), (ones16[:], d["cones"]),
                      (tri16[:], d["ctri"])],
             writes=[b_ident, b_ones, b_tri], tag="consts")

    stack = contextlib.ExitStack()

    def emit_layernorm(src_ap, ph):
        with contextlib.ExitStack() as es:
            xst = [es.enter_context(
                nc.sbuf_tensor(f"xst{ph}_{i}", [128, D], F32))
                for i in range(3)]
            lnd = es.enter_context(nc.sbuf_tensor(f"lnd{ph}", [128, D], BF16))
            ln16 = [es.enter_context(
                nc.sbuf_tensor(f"ln16{ph}_{i}", [128, D], BF16))
                for i in range(2)]
            cols = es.enter_context(
                nc.sbuf_tensor(f"lncol{ph}", [128, 8], F32))
            p_ln = [es.enter_context(
                nc.psum_tensor(f"pln{ph}_{i}", [128, 2048], BF16))
                for i in range(2)]
            b_xst = [Buf("xst0"), Buf("xst1"), Buf("xst2")]
            b_lnd = Buf("lnd")
            b_ln16 = [Buf("ln16_0"), Buf("ln16_1")]
            b_cols = Buf("lncol")
            b_pln = [Buf("pln0"), Buf("pln1")]
            HKT = 2048 // 128

            for tc in range(NT):
                par = tc % 3
                x_t, bx = xst[par], b_xst[par]
                dma_step("scalar",
                         [(x_t[:], src_ap[tc * 128:(tc + 1) * 128, :])],
                         writes=[bx], tag=f"ln{ph}.x{tc}")
                P.add("vector",
                      lambda e, x_t=x_t: e.reduce_sum(cols[:, 0:1], x_t[:],
                                                      axis=AX),
                      reads=[bx], writes=[b_cols], tag=f"ln{ph}.sum{tc}")
                P.add("scalar",
                      lambda e, x_t=x_t: e.activation(
                          lnd[:], x_t[:], AF.Square, accum_out=cols[:, 1:2]),
                      reads=[bx], writes=[b_lnd, b_cols],
                      tag=f"ln{ph}.sq{tc}")
                P.add("vector",
                      lambda e: e.tensor_scalar(cols[:, 2:3], cols[:, 0:1],
                                                1.0 / D, 0.0, ALU.mult,
                                                ALU.add),
                      reads=[b_cols], writes=[b_cols], tag=f"ln{ph}.mean{tc}")
                P.add("vector",
                      lambda e: e.tensor_tensor(cols[:, 3:4], cols[:, 2:3],
                                                cols[:, 2:3], ALU.mult),
                      reads=[b_cols], writes=[b_cols], tag=f"ln{ph}.m2{tc}")
                P.add("vector",
                      lambda e: e.scalar_tensor_tensor(
                          cols[:, 4:5], cols[:, 1:2], 1.0 / D, cols[:, 3:4],
                          ALU.mult, ALU.subtract),
                      reads=[b_cols], writes=[b_cols], tag=f"ln{ph}.var{tc}")
                P.add("vector",
                      lambda e: e.tensor_scalar(cols[:, 4:5], cols[:, 4:5],
                                                1e-5, 1.0, ALU.add, ALU.mult),
                      reads=[b_cols], writes=[b_cols], tag=f"ln{ph}.eps{tc}")
                P.add("scalar",
                      lambda e: e.activation(cols[:, 5:6], cols[:, 4:5],
                                             AF.Ln),
                      reads=[b_cols], writes=[b_cols], tag=f"ln{ph}.lnv{tc}")
                P.add("scalar",
                      lambda e: e.activation(cols[:, 6:7], cols[:, 5:6],
                                             AF.Exp, scale=-0.5),
                      reads=[b_cols], writes=[b_cols], tag=f"ln{ph}.rstd{tc}")
                P.add("vector",
                      lambda e: e.scalar_tensor_tensor(
                          cols[:, 7:8], cols[:, 2:3], -1.0, cols[:, 6:7],
                          ALU.mult, ALU.mult),
                      reads=[b_cols], writes=[b_cols], tag=f"ln{ph}.nb{tc}")
                l16, bl16 = ln16[tc % 2], b_ln16[tc % 2]
                P.add("vector",
                      lambda e, x_t=x_t, l16=l16: e.tensor_scalar(
                          l16[:], x_t[:], cols[:, 6:7], cols[:, 7:8],
                          ALU.mult, ALU.add),
                      reads=[bx, b_cols], writes=[bl16],
                      tag=f"ln{ph}.norm{tc}")
                nbat = (KD + HKT - 1) // HKT
                for bi in range(nbat):
                    k0 = bi * HKT
                    k1 = min(KD, k0 + HKT)
                    pp, bpp = p_ln[bi % 2], b_pln[bi % 2]

                    def em_tr(e, l16=l16, k0=k0, k1=k1, pp=pp):
                        last = None
                        for k in range(k0, k1):
                            last = e.transpose(
                                pp[:, (k - k0) * 128:(k - k0 + 1) * 128],
                                l16[:, k * 128:(k + 1) * 128], ident16[:])
                        return last
                    P.add("tensor", em_tr, reads=[bl16, b_ident],
                          writes=[bpp], tag=f"ln{ph}.tr{tc}.{bi}")

                    def em_cp(e, k0=k0, k1=k1, pp=pp, tc=tc):
                        dst = lnT3()[:, k0:k1, tc * 128:(tc + 1) * 128]
                        return e.tensor_copy(dst, pp[:, 0:(k1 - k0) * 128])
                    P.add("vector", em_cp, reads=[bpp], writes=[b_lnT],
                          tag=f"ln{ph}.cp{tc}.{bi}")

    emit_layernorm(d["x"], 1)
    P.barrier()

    with contextlib.ExitStack() as es:
        w16 = {m: es.enter_context(
            nc.sbuf_tensor(f"w16{m}", [128, KD * 128], BF16))
            for m in ("q", "k", "v")}
        b_w16 = {m: Buf(f"w16{m}") for m in ("q", "k", "v")}
        bh16 = [es.enter_context(
            nc.sbuf_tensor(f"bh16_{i}", [1, 256], BF16)) for i in range(2)]
        b_bh = [Buf("bh0"), Buf("bh1")]
        bv_col = [es.enter_context(
            nc.sbuf_tensor(f"bv_col{i}", [128, 1], F32)) for i in range(2)]
        b_bvc = [Buf("bv0"), Buf("bv1")]
        qTr = [es.enter_context(
            nc.sbuf_tensor(f"qT{i}", [128, T], BF16)) for i in range(2)]
        kTr = [es.enter_context(
            nc.sbuf_tensor(f"kT{i}", [128, T], BF16)) for i in range(2)]
        a16v = [es.enter_context(
            nc.sbuf_tensor(f"a16v{i}", [128, T], BF16)) for i in range(2)]
        vNr = [es.enter_context(
            nc.sbuf_tensor(f"vN{i}", [128, NT * 128], BF16))
            for i in range(2)]
        b_qT = [Buf("qT0"), Buf("qT1")]
        b_kT = [Buf("kT0"), Buf("kT1")]
        b_vN = [Buf("vN0"), Buf("vN1")]
        b_a16v = [Buf("a16v0"), Buf("a16v1")]
        acol = [es.enter_context(
            nc.sbuf_tensor(f"acol{i}", [128, 8], F32)) for i in range(2)]
        diag16 = [es.enter_context(
            nc.sbuf_tensor(f"diag16_{i}", [128, 128], BF16))
            for i in range(2)]
        b_acol = [Buf("ac0"), Buf("ac1")]
        b_diag = [Buf("dg0"), Buf("dg1")]
        attTb = es.enter_context(
            nc.sbuf_tensor("attTb", [128, NT * CH], BF16))
        b_attTb = Buf("attTb")
        oT = es.enter_context(nc.sbuf_tensor("oT", [128, T], BF16))
        b_oT = Buf("oT")
        p_qkv = es.enter_context(nc.psum_tensor("p_qkv", [128, 512], F32))
        b_pqkv = Buf("p_qkv")
        p_sc = es.enter_context(
            nc.psum_tensor("p_sc", [128, min(T, 2048)], F32))
        b_psc = [Buf("p_sc_lo"), Buf("p_sc_hi")]
        p_tr = es.enter_context(nc.psum_tensor("p_tr", [128, 1024], F32))
        b_ptr = [Buf("p_tr0"), Buf("p_tr1")]
        p_av = es.enter_context(nc.psum_tensor("p_av", [128, 512], F32))
        b_pav = Buf("p_av")

        def attTb3():
            return attTb[:].rearrange("p (t c) -> p t c", t=NT)

        def fetch_head_w(h):
            for m, wname in (("q", "wq"), ("k", "wk"), ("v", "wv")):
                dma_step("gpsimd", [(w16[m][:], d[wname][h])],
                         writes=[b_w16[m]], tag=f"h{h}.w{m}")
            hp = h % 2
            dma_step("gpsimd",
                     [(bh16[hp][:, 0:128], d["bq"][:, h * 128:(h + 1) * 128]),
                      (bh16[hp][:, 128:256],
                       d["bk"][:, h * 128:(h + 1) * 128])],
                     writes=[b_bh[hp]], tag=f"h{h}.b")
            dma_step("sync",
                     [(bv_col[hp][:], d["bvc"][h * 128:(h + 1) * 128, :])],
                     writes=[b_bvc[hp]], tag=f"h{h}.bv")

        def qkv_units(h):
            hp = h % 2
            units = []
            for m, dstT, bdst, bias_sl, scale in (
                    ("q", qTr[hp], b_qT[hp], (0, 128),
                     1.0 / float(np.sqrt(HD))),
                    ("k", kTr[hp], b_kT[hp], (128, 256), 1.0),
                    ("v", a16v[hp], b_a16v[hp], None, 1.0)):
                for tch in range(NT5):
                    def unit(m=m, dstT=dstT, bdst=bdst, bias_sl=bias_sl,
                             scale=scale, tch=tch, h=h, hp=hp):
                        def em_mm(e):
                            last = None
                            for k in range(KD):
                                last = e.matmul(
                                    p_qkv[:, 0:TC5],
                                    w16[m][:, k * 128:(k + 1) * 128],
                                    lnT3()[:, k, tch * TC5:(tch + 1) * TC5],
                                    start=(k == 0),
                                    stop=(bias_sl is None and k == KD - 1))
                            if bias_sl is not None:
                                last = e.matmul(
                                    p_qkv[:, 0:TC5],
                                    bh16[hp][:, bias_sl[0]:bias_sl[1]],
                                    ones16[:, 0:TC5],
                                    start=False, stop=True)
                            return last
                        reads = [b_w16[m], b_lnT, b_ones]
                        if bias_sl is not None:
                            reads.append(b_bh[hp])
                        P.add("tensor", em_mm, reads=reads, writes=[b_pqkv],
                              tag=f"h{h}.{m}mm{tch}")
                        P.add("vector",
                              lambda e, dstT=dstT, tch=tch, scale=scale:
                              e.tensor_scalar(
                                  dstT[:, tch * TC5:(tch + 1) * TC5],
                                  p_qkv[:, 0:TC5], scale, 0.0,
                                  ALU.mult, ALU.add),
                              reads=[b_pqkv], writes=[bdst],
                              tag=f"h{h}.{m}ev{tch}")
                    units.append(unit)
            for bi in range((NT + 3) // 4):
                def unit(bi=bi, h=h, hp=hp):
                    i0v, i1v = bi * 4, min(NT, bi * 4 + 4)
                    half = bi % 2
                    bptr = b_ptr[half]

                    def em_vtr(e):
                        last = None
                        for i in range(i0v, i1v):
                            last = e.matmul(
                                p_tr[:, half * 512 + (i - i0v) * 128:
                                     half * 512 + (i - i0v + 1) * 128],
                                a16v[hp][:, i * 128:(i + 1) * 128],
                                ident16[:], start=True, stop=True)
                        return last
                    P.add("tensor", em_vtr, reads=[b_a16v[hp], b_ident],
                          writes=[bptr], tag=f"h{h}.vtr{bi}")

                    def em_vcp(e):
                        return e.tensor_copy(
                            vNr[hp][:, i0v * 128:i1v * 128],
                            p_tr[:, half * 512:half * 512 +
                                 (i1v - i0v) * 128])
                    P.add("vector", em_vcp, reads=[bptr],
                          writes=[b_vN[hp]], tag=f"h{h}.vcp{bi}")
                units.append(unit)
            return units

        fetch_head_w(0)
        for u in qkv_units(0):
            u()
        for h in range(H):
            hp = h % 2
            qT, kT, vN = qTr[hp], kTr[hp], vNr[hp]
            bqT, bkT, bvN = b_qT[hp], b_kT[hp], b_vN[hp]
            att16 = [a16v[hp]]
            b_a16 = [b_a16v[hp]]
            if h + 1 < H:
                fetch_head_w(h + 1)
                nxt = qkv_units(h + 1)
            else:
                nxt = []
            n_nxt0 = len(nxt)
            tidx = 0
            for c in range(NCH):
                pend = [None]
                for il in range(TPC):
                    ig = c * TPC + il
                    W = (ig + 1) * 128
                    par = ig % 2
                    a16s, ac, dg = att16[0], acol[par], diag16[par]
                    ba16, bac, bdg = b_a16[0], b_acol[par], b_diag[par]
                    HSC = min(T, 2048) // 2
                    if W <= HSC:
                        base = HSC * (ig % 2)
                        bpsc = [b_psc[ig % 2]]
                    else:
                        base = 0
                        bpsc = b_psc
                    chunks = [(tkc, min(512, W - tkc))
                              for tkc in range(0, W, 512)]
                    nch = len(chunks)
                    for ci, (tkc, wdt) in enumerate(chunks):
                        hb = ([b_psc[(base + tkc) // HSC]]
                              if wdt + tkc + base <= HSC or base + tkc >= HSC
                              else bpsc)

                        def em_scc(e, ig=ig, tkc=tkc, wdt=wdt, base=base,
                                   qT=qT, kT=kT):
                            diag_in = tkc <= ig * 128 < tkc + wdt
                            last = e.matmul(
                                p_sc[:, base + tkc:base + tkc + wdt],
                                qT[:, ig * 128:(ig + 1) * 128],
                                kT[:, tkc:tkc + wdt],
                                start=True, stop=not diag_in)
                            if diag_in:
                                last = e.matmul(
                                    p_sc[:, base + ig * 128:
                                         base + (ig + 1) * 128],
                                    ident16[:], tri16[:],
                                    start=False, stop=True)
                            return last
                        P.add("tensor", em_scc,
                              reads=[bqT, bkT, b_ident, b_tri],
                              writes=hb, tag=f"h{h}.sc{ig}.{ci}")
                        if nch == 1:
                            P.add("vector",
                                  lambda e, W=W, ac=ac, base=base:
                                  e.reduce_max(ac[:, 0:1],
                                               p_sc[:, base:base + W],
                                               axis=AX, negate=True),
                                  reads=hb, writes=[bac],
                                  tag=f"h{h}.mx{ig}")
                        else:
                            P.add("vector",
                                  lambda e, ci=ci, tkc=tkc, wdt=wdt,
                                  ac=ac, base=base:
                                  e.reduce_max(ac[:, 4 + ci:5 + ci],
                                               p_sc[:, base + tkc:
                                                    base + tkc + wdt],
                                               axis=AX),
                                  reads=hb, writes=[bac],
                                  tag=f"h{h}.mxc{ig}.{ci}")
                        if ci == 0 and pend[0] is not None:
                            pend[0]()
                            pend[0] = None
                            tidx += 1
                            if tidx >= 2:
                                share = (n_nxt0 * (tidx - 1)) // \
                                    max(1, NT - 2)
                                while nxt and n_nxt0 - len(nxt) < share:
                                    nxt.pop(0)()
                    if nch > 1:
                        P.add("vector",
                              lambda e, ac=ac, nch=nch: e.reduce_max(
                                  ac[:, 0:1], ac[:, 4:4 + nch], axis=AX,
                                  negate=True),
                              reads=[bac], writes=[bac],
                              tag=f"h{h}.mx{ig}")
                    if W <= HSC:
                        P.add("scalar",
                              lambda e, W=W, a16s=a16s, ac=ac, base=base:
                              e.activation(
                                  a16s[:, 0:W], p_sc[:, base:base + W],
                                  AF.Exp, bias=ac[:, 0:1],
                                  accum_out=ac[:, 1:2]),
                              reads=bpsc + [bac], writes=[ba16, bac],
                              tag=f"h{h}.ex{ig}")
                        P.add("vector",
                              lambda e, ac=ac: e.reciprocal(ac[:, 2:3],
                                                            ac[:, 1:2]),
                              reads=[bac], writes=[bac],
                              tag=f"h{h}.rc{ig}")
                    else:
                        P.add("scalar",
                              lambda e, a16s=a16s, ac=ac, HSC=HSC:
                              e.activation(
                                  a16s[:, 0:HSC], p_sc[:, 0:HSC],
                                  AF.Exp, bias=ac[:, 0:1],
                                  accum_out=ac[:, 1:2]),
                              reads=[b_psc[0], bac], writes=[ba16, bac],
                              tag=f"h{h}.exlo{ig}")
                        P.add("scalar",
                              lambda e, W=W, a16s=a16s, ac=ac, HSC=HSC:
                              e.activation(
                                  a16s[:, HSC:W], p_sc[:, HSC:W],
                                  AF.Exp, bias=ac[:, 0:1],
                                  accum_out=ac[:, 3:4]),
                              reads=[b_psc[1], bac], writes=[ba16, bac],
                              tag=f"h{h}.exhi{ig}")
                        P.add("vector",
                              lambda e, ac=ac: e.scalar_tensor_tensor(
                                  ac[:, 1:2], ac[:, 1:2], 1.0, ac[:, 3:4],
                                  ALU.mult, ALU.add),
                              reads=[bac], writes=[bac],
                              tag=f"h{h}.zs{ig}")
                        P.add("vector",
                              lambda e, ac=ac: e.reciprocal(ac[:, 2:3],
                                                            ac[:, 1:2]),
                              reads=[bac], writes=[bac],
                              tag=f"h{h}.rc{ig}")
                    P.add("vector",
                          lambda e, ac=ac, dg=dg: e.tensor_scalar(
                              dg[:], ident16[:], ac[:, 2:3], 0.0,
                              ALU.mult, ALU.add),
                          reads=[bac, b_ident], writes=[bdg],
                          tag=f"h{h}.dg{ig}")

                    def make_flush(ig=ig, il=il, c=c, a16s=a16s, dg=dg,
                                   ba16=ba16, bdg=bdg):
                        def flush():
                            ntile = ig + 1
                            for bi in range((ntile + 3) // 4):
                                k0, k1 = bi * 4, min(ntile, bi * 4 + 4)
                                half = bi % 2
                                bptr = b_ptr[half]

                                def em_atr(e, a16s=a16s, dg=dg, k0=k0,
                                           k1=k1, half=half):
                                    last = None
                                    for kt in range(k0, k1):
                                        last = e.matmul(
                                            p_tr[:, half * 512 +
                                                 (kt - k0) * 128:
                                                 half * 512 +
                                                 (kt - k0 + 1) * 128],
                                            a16s[:, kt * 128:(kt + 1) * 128],
                                            dg[:], start=True, stop=True)
                                    return last
                                P.add("tensor", em_atr,
                                      reads=[ba16, bdg], writes=[bptr],
                                      tag=f"h{h}.atr{ig}.{bi}")

                                if ig + 1 > 8:
                                    def em_acp(e, k0=k0, k1=k1, half=half,
                                               il=il):
                                        srcp = p_tr[:, half * 512:
                                                    half * 512 +
                                                    (k1 - k0) * 128]
                                        dst = attTb3()[:, k0:k1,
                                                       il * 128:
                                                       (il + 1) * 128]
                                        return e.tensor_copy(dst, srcp)
                                    P.add("vector", em_acp, reads=[bptr],
                                          writes=[b_attTb],
                                          tag=f"h{h}.acp{ig}.{bi}")
                                else:
                                    def em_acp(e, k0=k0, k1=k1, half=half,
                                               il=il):
                                        srcp = p_tr[:, half * 512:
                                                    half * 512 +
                                                    (k1 - k0) * 128]
                                        dst = attTb3()[:, k0:k1,
                                                       il * 128:
                                                       (il + 1) * 128]
                                        return e.copy(dst, srcp)
                                    P.add("scalar", em_acp, reads=[bptr],
                                          writes=[b_attTb],
                                          tag=f"h{h}.acp{ig}.{bi}")
                            ntail = (c + 1) * TPC - ntile
                            if ntail > 0:
                                def em_z(e, ntile=ntile, ntail=ntail,
                                         il=il):
                                    dst = attTb3()[:, ntile:ntile + ntail,
                                                   il * 128:(il + 1) * 128]
                                    return e.memset(dst, 0.0)
                                P.add("gpsimd", em_z, writes=[b_attTb],
                                      tag=f"h{h}.zz{ig}")
                        return flush
                    pend[0] = make_flush()
                if pend[0] is not None:
                    pend[0]()
                    pend[0] = None

                nkt = (c + 1) * TPC

                def em_av(e, c=c, nkt=nkt, vN=vN):
                    last = None
                    for kt in range(nkt):
                        last = e.matmul(p_av[:, 0:CH],
                                        vN[:, kt * 128:(kt + 1) * 128],
                                        attTb3()[:, kt, 0:CH],
                                        start=(kt == 0), stop=(kt == nkt - 1))
                    return last
                P.add("tensor", em_av, reads=[bvN, b_attTb], writes=[b_pav],
                      tag=f"h{h}.av{c}")

                def em_oev(e, c=c, h=h):
                    return e.activation(oT[:, c * CH:(c + 1) * CH],
                                        p_av[:, 0:CH], AF.Identity,
                                        bias=bv_col[h % 2][:], scale=1.0)
                P.add("scalar", em_oev, reads=[b_pav, b_bvc[h % 2]],
                      writes=[b_oT], tag=f"h{h}.oev{c}")

            for u in nxt:
                u()
            dma_step("sync", [(d["attnT"][h], oT[:])], reads=[b_oT],
                     tag=f"h{h}.osp")

        P.barrier()

    with contextlib.ExitStack() as es:
        at_sb = [es.enter_context(
            nc.sbuf_tensor(f"at_sb{i}", [128, KD * 128], BF16))
            for i in range(3)]
        b_at = [Buf(f"at{i}") for i in range(3)]
        xo_sb = [es.enter_context(
            nc.sbuf_tensor(f"xo_sb{i}", [128, JH], F32)) for i in range(2)]
        b_xo = [Buf("xo0"), Buf("xo1")]
        x1_sb = [es.enter_context(
            nc.sbuf_tensor(f"x1_sb{i}", [128, JH], F32)) for i in range(2)]
        b_x1 = [Buf("x1o0"), Buf("x1o1")]
        bo16 = es.enter_context(nc.sbuf_tensor("bo16", [1, D], BF16))
        b_bo = Buf("bo16")
        p_o = [es.enter_context(
            nc.psum_tensor(f"p_o{i}", [128, min(JH, 2048)], F32))
            for i in range(2)]
        b_po = [Buf("po0"), Buf("po1")]

        dma_step("gpsimd", [(bo16[:], d["bo"])], writes=[b_bo], tag="bo")

        OPROJ_PING = 2 * JH <= LNW
        if OPROJ_PING:
            b_woq = [[Buf(f"woq{q}_{i}") for i in range(4)]
                     for q in range(2)]
            QOF = JH
        else:
            _bw = [Buf("woq_0")] * 4
            b_woq = [_bw, _bw]
            QOF = 0

        for jh in range(NJH):
            qp = jh % 2
            ksp = max(1, KD // 4)
            for si in range(0, KD, ksp):
                pairs = [(lnT3()[:, k, qp * QOF:qp * QOF + JH],
                          d["wo"][k, :, jh * JH:(jh + 1) * JH])
                         for k in range(si, min(KD, si + ksp))]
                dma_step("gpsimd", pairs, writes=[b_woq[qp][si // ksp]],
                         tag=f"op.w{jh}.{si}")
            for tc in range(NT):
                par = tc % 3
                a_s, b_as = at_sb[par], b_at[par]
                x_s, b_xs = xo_sb[tc % 2], b_xo[tc % 2]
                x1s, b_x1s = x1_sb[tc % 2], b_x1[tc % 2]
                po, b_pos = p_o[tc % 2], b_po[tc % 2]
                npart = (KD + 7) // 8
                pairs = []
                a3 = a_s[:].rearrange("p (t j) -> p t j", t=KD)
                for gi in range(npart):
                    k0, k1 = gi * 8, min(KD, gi * 8 + 8)
                    src = d["attnT"][k0:k1, :, tc * 128:(tc + 1) * 128]
                    pairs.append((a3[:, k0:k1, :], src.transpose([1, 0, 2])))
                dma_step("sync", pairs, writes=[b_as], tag=f"op.at{jh}.{tc}")
                dma_step("scalar",
                         [(x_s[:], d["x"][tc * 128:(tc + 1) * 128,
                                          jh * JH:(jh + 1) * JH])],
                         writes=[b_xs], tag=f"op.x{jh}.{tc}")

                def em_omm(e, a_s=a_s, po=po, jh=jh, qp=qp):
                    last = None
                    for k in range(KD):
                        for jc in range(0, JH, 512):
                            wdt = min(512, JH - jc)
                            last = e.matmul(
                                po[:, jc:jc + wdt],
                                a_s[:, k * 128:(k + 1) * 128],
                                lnT3()[:, k, qp * QOF + jc:
                                       qp * QOF + jc + wdt],
                                start=(k == 0), stop=False)
                    for jc in range(0, JH, 512):
                        wdt = min(512, JH - jc)
                        last = e.matmul(
                            po[:, jc:jc + wdt], ones16[:, 0:128],
                            bo16[:, jh * JH + jc:jh * JH + jc + wdt],
                            start=False, stop=True)
                    return last
                P.add("tensor", em_omm,
                      reads=[b_as, b_ones, b_bo] + list(b_woq[qp]),
                      writes=[b_pos], tag=f"op.mm{jh}.{tc}")

                def em_oadd(e, x1s=x1s, po=po, x_s=x_s):
                    return e.tensor_tensor(x1s[:], po[:, 0:JH], x_s[:],
                                           ALU.add)
                P.add("vector", em_oadd, reads=[b_pos, b_xs], writes=[b_x1s],
                      tag=f"op.add{jh}.{tc}")
                dma_step("sync",
                         [(d["x1"][tc * 128:(tc + 1) * 128,
                                   jh * JH:(jh + 1) * JH], x1s[:])],
                         reads=[b_x1s], tag=f"op.st{jh}.{tc}")
        P.barrier()

    emit_layernorm(d["x1"], 2)
    P.barrier()

    with contextlib.ExitStack() as es:
        hot32 = es.enter_context(
            nc.sbuf_tensor("hot32", [KD, KD * 128], BF16))
        b_h32 = Buf("hot32")
        conehot = es.enter_context(
            nc.sbuf_tensor("conehot", [128, KD * KD], BF16))
        b_coh = Buf("conehot")
        dma_step("sync", [(hot32[:], d["chot32"]),
                          (conehot[:], d["cconehot"])],
                 writes=[b_h32, b_coh], tag="hots")
        xgT = es.enter_context(nc.sbuf_tensor("xgT", [KD, T], BF16))
        b_xgT = Buf("xgT")
        wstb = [es.enter_context(
            nc.sbuf_tensor(f"wst{i}", [128, KD * SJ], BF16))
            for i in range(2)]
        b_wst = [Buf("wst0"), Buf("wst1")]
        KH = KD // 2 if KD % 2 == 0 else KD
        nibgu = es.enter_context(
            nc.sbuf_tensor("nibgu", [128, KH * SJ], I32))
        b_nib = Buf("nibgu")
        pk_sb, b_pk = {}, {}
        ssch, nzs, znib, qzst, b_ss, b_nzs, b_zs, b_qz = ({}, {}, {}, {},
                                                          {}, {}, {}, {})
        for m in ("g", "u"):
            pk_sb[m] = [es.enter_context(
                nc.sbuf_tensor(f"pk{m}{i}", [128, KD * PC], I32))
                for i in range(2)]
            ssch[m] = [es.enter_context(
                nc.sbuf_tensor(f"ssch{m}{i}", [KD, SJ], BF16))
                for i in range(2)]
            nzs[m] = [es.enter_context(
                nc.sbuf_tensor(f"nzs{m}{i}", [KD, SJ], BF16))
                for i in range(2)]
            znib[m] = [es.enter_context(
                nc.sbuf_tensor(f"znib{m}{i}", [KD, SJ], I32))
                for i in range(2)]
            qzst[m] = [es.enter_context(
                nc.sbuf_tensor(f"qzst{m}{i}", [KD, PC], I32))
                for i in range(2)]
            b_pk[m] = [Buf(f"pk{m}{i}") for i in range(2)]
            b_ss[m] = [Buf(f"ss{m}{i}") for i in range(2)]
            b_nzs[m] = [Buf(f"nzs{m}{i}") for i in range(2)]
            b_zs[m] = [Buf(f"zs{m}{i}") for i in range(2)]
            b_qz[m] = [Buf(f"qz{m}{i}") for i in range(2)]
        silu_sb = es.enter_context(nc.sbuf_tensor("silu_sb", [128, T], BF16))
        b_silu = Buf("silu")
        hTb = [es.enter_context(
            nc.sbuf_tensor(f"hTb{i}", [128, T], BF16)) for i in range(2)]
        b_hTb = [Buf("hTb0"), Buf("hTb1")]
        p_gu = {m: es.enter_context(
            nc.psum_tensor(f"p_{m}", [128, 1024], F32)) for m in ("g", "u")}
        b_pgu = {m: Buf(f"p_{m}") for m in ("g", "u")}
        p_s = [es.enter_context(
            nc.psum_tensor(f"p_s{i}", [128, 512], F32)) for i in range(4)]
        b_ps = [Buf(f"ps{i}") for i in range(4)]

        state = {"bc": 0}

        NT5X = (T + 511) // 512
        for tch in range(NT5X):
            w5 = min(512, T - tch * 512)

            def em_xg(e, tch=tch, w5=w5):
                last = None
                for g in range(KD):
                    last = e.matmul(
                        p_s[tch % 4][0:KD, 0:w5],
                        conehot[:, g * KD:(g + 1) * KD],
                        lnT3()[:, g, tch * 512:tch * 512 + w5],
                        start=(g == 0), stop=(g == KD - 1))
                return last
            P.add("tensor", em_xg, reads=[b_coh, b_lnT],
                  writes=[b_ps[tch % 4]], tag=f"xg.mm{tch}")
            P.add("vector",
                  lambda e, tch=tch, w5=w5: e.tensor_copy(
                      xgT[0:KD, tch * 512:tch * 512 + w5],
                      p_s[tch % 4][0:KD, 0:w5]),
                  reads=[b_ps[tch % 4]], writes=[b_xgT], tag=f"xg.cp{tch}")

        NKH = (KD + KH - 1) // KH
        n_main = NST * 2

        def deq_stage_parts(sidx):
            s = sidx // 2
            m = "g" if sidx % 2 == 0 else "u"
            par = sidx % 2
            sp = s % 2
            dma_step("sync", [(pk_sb[m][sp][:], d[m + "qw"][s]),
                              (qzst[m][sp][:], d[m + "qz"][s])],
                     writes=[b_pk[m][sp], b_qz[m][sp]], tag=f"{m}{s}.pk")
            dma_step("gpsimd", [(ssch[m][sp][:], d[m + "sc"][s])],
                     writes=[b_ss[m][sp]], tag=f"{m}{s}.sc")

            def zero_part():
                def em_uz(e, m=m, sp=sp):
                    last = None
                    for i in range(8):
                        last = e.tensor_scalar(
                            znib[m][sp][0:KD, PI[i]:SJ:8], qzst[m][sp][:],
                            4 * i, 15, ALU.logical_shift_right,
                            ALU.bitwise_and)
                    return last
                P.add("vector", em_uz, reads=[b_qz[m][sp]],
                      writes=[b_zs[m][sp]], tag=f"{m}{s}.upz")
                P.add("vector",
                      lambda e, m=m, sp=sp: e.scalar_tensor_tensor(
                          nzs[m][sp][:], znib[m][sp][0:KD, :], -1.0,
                          ssch[m][sp][:], ALU.mult, ALU.mult),
                      reads=[b_zs[m][sp], b_ss[m][sp]],
                      writes=[b_nzs[m][sp]], tag=f"{m}{s}.nzs")

            GB = max(1, 512 // SJ)

            def half_part(kh):
                h0 = kh * KH
                h1 = min(KD, h0 + KH)

                def em_uw(e, m=m, h0=h0, h1=h1, sp=sp):
                    last = None
                    w3 = nibgu[:].rearrange("p (g j) -> p g j", g=KH)
                    for i in range(8):
                        last = e.tensor_scalar(
                            w3[:, 0:h1 - h0, PI[i]:SJ:8],
                            pk_sb[m][sp][:, h0 * PC:h1 * PC],
                            4 * i, 15, ALU.logical_shift_right,
                            ALU.bitwise_and)
                    return last
                P.add("vector", em_uw, reads=[b_pk[m][sp]],
                      writes=[b_nib], tag=f"{m}{s}.upw{kh}")

                for b0 in range(h0, h1, GB):
                    b1 = min(h1, b0 + GB)
                    bp = state["bc"]
                    state["bc"] = (bp + 1) % 4

                    def em_bc(e, b0=b0, b1=b1, m=m, bp=bp, sp=sp):
                        last = None
                        for g in range(b0, b1):
                            last = e.matmul(
                                p_s[bp][:, (g - b0) * SJ:(g - b0 + 1) * SJ],
                                hot32[0:KD, g * 128:(g + 1) * 128],
                                ssch[m][sp][0:KD, :], start=True, stop=True)
                        return last
                    P.add("tensor", em_bc,
                          reads=[b_h32, b_ss[m][sp]],
                          writes=[b_ps[bp]], tag=f"{m}{s}.bc{b0}")
                    nsl = nibgu[:, (b0 - h0) * SJ:(b1 - h0) * SJ]
                    sl = wstb[par][:, b0 * SJ:b1 * SJ]
                    P.add("vector",
                          lambda e, sl=sl, nsl=nsl, b0=b0, b1=b1, bp=bp:
                          e.tensor_tensor(
                              sl, nsl, p_s[bp][:, 0:(b1 - b0) * SJ],
                              ALU.mult),
                          reads=[b_ps[bp], b_nib, b_wst[par]],
                          writes=[b_wst[par]], tag=f"{m}{s}.mul{b0}")

            parts = [lambda: (zero_part(), half_part(0))]
            for kh in range(1, NKH):
                parts.append(lambda kh=kh: half_part(kh))
            return parts

        def mm_pr(sidx, pr):
            s = sidx // 2
            m = "g" if sidx % 2 == 0 else "u"
            par = sidx % 2
            sp = s % 2
            prw = min(1024, T - pr)

            def em_mm(e, m=m, par=par, pr=pr, prw=prw, sp=sp):
                last = None
                for k in range(KD):
                    for tci in range(0, prw, 512):
                        wdt = min(512, prw - tci)
                        last = e.matmul(
                            p_gu[m][:, tci:tci + wdt],
                            wstb[par][:, k * SJ:(k + 1) * SJ],
                            lnT3()[:, k, pr + tci:pr + tci + wdt],
                            start=(k == 0), stop=False)
                for tci in range(0, prw, 512):
                    wdt = min(512, prw - tci)
                    last = e.matmul(
                        p_gu[m][:, tci:tci + wdt], nzs[m][sp][:],
                        xgT[0:KD, pr + tci:pr + tci + wdt],
                        start=False, stop=True)
                return last
            P.add("tensor", em_mm,
                  reads=[b_wst[par], b_lnT, b_nzs[m][sp], b_xgT],
                  writes=[b_pgu[m]], tag=f"{m}{s}.mm{pr}")
            if m == "g":
                P.add("scalar",
                      lambda e, pr=pr, prw=prw: e.activation(
                          silu_sb[:, pr:pr + prw], p_gu["g"][:, 0:prw],
                          AF.Silu),
                      reads=[b_pgu["g"]], writes=[b_silu],
                      tag=f"{m}{s}.si{pr}")
            else:
                hp = s % 2
                P.add("vector",
                      lambda e, pr=pr, prw=prw, hp=hp:
                      e.scalar_tensor_tensor(
                          hTb[hp][:, pr:pr + prw], p_gu["u"][:, 0:prw],
                          1.0, silu_sb[:, pr:pr + prw],
                          ALU.mult, ALU.mult),
                      reads=[b_pgu["u"], b_silu], writes=[b_hTb[hp]],
                      tag=f"{m}{s}.hm{pr}")

        INTERLEAVE_DEQ = True
        prs = list(range(0, T, 1024))
        pending = deq_stage_parts(0)
        for p in pending:
            p()
        for sidx in range(n_main):
            s = sidx // 2
            m = "g" if sidx % 2 == 0 else "u"
            nxt = deq_stage_parts(sidx + 1) if sidx + 1 < n_main else []
            for pi, pr in enumerate(prs):
                mm_pr(sidx, pr)
                if INTERLEAVE_DEQ:
                    share = (len(nxt) * (pi + 1)) // max(1, len(prs))
                    while len(nxt) and share > 0:
                        nxt.pop(0)()
                        share -= 1
            for p in nxt:
                p()
            if m == "u":
                hp = s % 2
                dma_step("sync", [(d["hT"][s], hTb[hp][:])],
                         reads=[b_hTb[hp]], tag=f"hT.st{s}")
        P.barrier()

    outer_es.close()

    with contextlib.ExitStack() as es:
        wdn_sb = es.enter_context(
            nc.sbuf_tensor("wdn_sb", [128, JG * DNW], BF16))
        b_wg = [Buf(f"wdn{g}") for g in range(JG)]
        hT_sb = [es.enter_context(
            nc.sbuf_tensor(f"hT_sb{i}", [128, TPR * 128], BF16))
            for i in range(6)]
        b_hts = [Buf(f"hts{i}") for i in range(6)]
        x1t = [es.enter_context(
            nc.sbuf_tensor(f"x1t{i}", [128, DNW], F32)) for i in range(4)]
        b_x1t = [Buf(f"x1t{i}") for i in range(4)]
        ot_sb = [es.enter_context(
            nc.sbuf_tensor(f"ot_sb{i}", [128, DNW], F32)) for i in range(4)]
        b_ot = [Buf(f"ot{i}") for i in range(4)]
        DGB = 4
        dpk = [es.enter_context(
            nc.sbuf_tensor(f"dpk{i}", [128, DGB * (DNW // 8)], I32))
            for i in range(2)]
        dnib = [es.enter_context(
            nc.sbuf_tensor(f"dnib{i}", [128, DGB * DNW], I32))
            for i in range(2)]
        dssc = [es.enter_context(
            nc.sbuf_tensor(f"dssc{i}", [JG, DNW], BF16)) for i in range(2)]
        dzsc = [es.enter_context(
            nc.sbuf_tensor(f"dzsc{i}", [JG, DNW], BF16)) for i in range(2)]
        dznb = [es.enter_context(
            nc.sbuf_tensor(f"dznb{i}", [JG, DNW], I32)) for i in range(2)]
        dqzs = [es.enter_context(
            nc.sbuf_tensor(f"dqzs{i}", [JG, DNW // 8], I32))
            for i in range(2)]
        hotsl = [es.enter_context(
            nc.sbuf_tensor(f"hotsl{i}", [JG, 128], BF16)) for i in range(4)]
        b_hsl = [Buf(f"hsl{i}") for i in range(4)]
        b_dpk = [Buf("dpk0"), Buf("dpk1")]
        b_dnib = [Buf("dnib0"), Buf("dnib1")]
        b_dss = [Buf("dss0"), Buf("dss1")]
        b_dzs = [Buf("dzs0"), Buf("dzs1")]
        b_dznb = [Buf("dznb0"), Buf("dznb1")]
        b_dqz = [Buf("dqz0"), Buf("dqz1")]
        p_dn = [es.enter_context(
            nc.psum_tensor(f"p_dn{i}", [128, 512], F32)) for i in range(TPR)]
        b_pdn = [Buf(f"pdn{i}") for i in range(TPR)]
        p_zs = [es.enter_context(
            nc.psum_tensor(f"p_zs{i}", [128, 512], F32)) for i in range(4)]
        b_pzs = [Buf(f"pzs{i}") for i in range(4)]

        def emit_down_unit(nci, g):
            np_ = nci % 2
            if g == 0:
                dma_step("gpsimd", [(dssc[np_][:], d["dsc"][nci])],
                         writes=[b_dss[np_]], tag=f"dn.sc{nci}")
                dma_step("scalar", [(dqzs[np_][:], d["dqz"][nci])],
                         writes=[b_dqz[np_]], tag=f"dn.qz{nci}")

                def em_uz(e, np_=np_):
                    last = None
                    for i in range(8):
                        last = e.tensor_scalar(
                            dznb[np_][0:JG, PI[i]:DNW:8], dqzs[np_][:],
                            4 * i, 15, ALU.logical_shift_right,
                            ALU.bitwise_and)
                    return last
                P.add("vector", em_uz, reads=[b_dqz[np_]],
                      writes=[b_dznb[np_]], tag=f"dn.uz{nci}")
                P.add("vector",
                      lambda e, np_=np_: e.tensor_copy(
                          dzsc[np_][:], dznb[np_][0:JG, :]),
                      reads=[b_dznb[np_]], writes=[b_dzs[np_]],
                      tag=f"dn.cz{nci}")
            dp = (g // DGB) % 2
            if g % DGB == 0:
                g1 = min(JG, g + DGB)
                ng = g1 - g
                pcw = DNW // 8
                dma_step("scalar",
                         [(dpk[dp][:, 0:ng * pcw],
                           d["dqw"][nci, :, g * pcw:g1 * pcw])],
                         writes=[b_dpk[dp]], tag=f"dn.pk{nci}.{g}")

                def em_uw(e, ng=ng, pcw=pcw, dp=dp):
                    last = None
                    w3 = dnib[dp][:].rearrange("p (g w) -> p g w", g=DGB)
                    for i in range(8):
                        last = e.tensor_scalar(
                            w3[:, 0:ng, PI[i]:DNW:8],
                            dpk[dp][:, 0:ng * pcw],
                            4 * i, 15, ALU.logical_shift_right,
                            ALU.bitwise_and)
                    return last
                P.add("vector", em_uw, reads=[b_dpk[dp]],
                      writes=[b_dnib[dp]], tag=f"dn.uw{nci}.{g}")
            hp = g % 4
            dma_step("scalar",
                     [(hotsl[hp][:],
                       d["chotJG"][:, g * 128:(g + 1) * 128])],
                     writes=[b_hsl[hp]], tag=f"dn.hs{nci}.{g}")
            zp = 2 * (g % 2)

            def em_bc(e, hp=hp, zp=zp, np_=np_):
                e.matmul(p_zs[zp][:, 0:DNW], hotsl[hp][:],
                         dzsc[np_][0:JG, 0:DNW], start=True, stop=True)
                return e.matmul(p_zs[zp + 1][:, 0:DNW], hotsl[hp][:],
                                dssc[np_][0:JG, 0:DNW],
                                start=True, stop=True)
            P.add("tensor", em_bc,
                  reads=[b_hsl[hp], b_dzs[np_], b_dss[np_]],
                  writes=[b_pzs[zp], b_pzs[zp + 1]], tag=f"dn.bc{nci}.{g}")
            nsl = dnib[dp][:, (g % DGB) * DNW:(g % DGB + 1) * DNW]
            sl = wdn_sb[:, g * DNW:(g + 1) * DNW]
            P.add("vector",
                  lambda e, sl=sl, nsl=nsl, zp=zp: e.tensor_tensor(
                      sl, nsl, p_zs[zp][:, 0:DNW], ALU.subtract),
                  reads=[b_pzs[zp], b_dnib[dp]], writes=[b_wg[g]],
                  tag=f"dn.sub{nci}.{g}")
            P.add("vector",
                  lambda e, sl=sl, zp=zp: e.tensor_tensor(
                      sl, sl, p_zs[zp + 1][:, 0:DNW], ALU.mult),
                  reads=[b_pzs[zp + 1]], writes=[b_wg[g]],
                  tag=f"dn.mul{nci}.{g}")

        for g in range(JG):
            emit_down_unit(0, g)
        for nci in range(NC_DN):
            for tr in range(NTR):
                for g in range(JG):
                    par = g % 6
                    hs, bhs = hT_sb[par], b_hts[par]
                    dma_step("sync",
                             [(hs[:], d["hT"][g][:, tr * TPR * 128:
                                                 (tr + 1) * TPR * 128])],
                             writes=[bhs], tag=f"dm.h{nci}.{tr}.{g}")

                    def em_mm(e, g=g, hs=hs):
                        last = None
                        for ti in range(TPR):
                            last = e.matmul(
                                p_dn[ti][:, 0:DNW],
                                hs[:, ti * 128:(ti + 1) * 128],
                                wdn_sb[:, g * DNW:(g + 1) * DNW],
                                start=(g == 0), stop=(g == JG - 1))
                        return last
                    P.add("tensor", em_mm, reads=[bhs, b_wg[g]],
                          writes=b_pdn, tag=f"dm.mm{nci}.{tr}.{g}")
                    if tr == NTR - 1 and nci + 1 < NC_DN:
                        emit_down_unit(nci + 1, g)
                for ti in range(TPR):
                    tc = tr * TPR + ti
                    qpar = ti % 4
                    xs, bxs = x1t[qpar], b_x1t[qpar]
                    os_, bos = ot_sb[qpar], b_ot[qpar]
                    dma_step("scalar",
                             [(xs[:], d["x1"][tc * 128:(tc + 1) * 128,
                                              nci * DNW:(nci + 1) * DNW])],
                             writes=[bxs], tag=f"dm.x{nci}.{tc}")
                    P.add("vector",
                          lambda e, ti=ti, os_=os_, xs=xs: e.tensor_tensor(
                              os_[:], p_dn[ti][:, 0:DNW], xs[:], ALU.add),
                          reads=[b_pdn[ti], bxs], writes=[bos],
                          tag=f"dm.add{nci}.{tc}")
                    dma_step("sync",
                             [(d["out"][tc * 128:(tc + 1) * 128,
                                        nci * DNW:(nci + 1) * DNW], os_[:])],
                             reads=[bos], tag=f"dm.st{nci}.{tc}")


    finals = [(DMA_SEM[q] + str(r), 16 * P.dma_count[q][r])
              for q in DMA_SEM for r in range(NLANE)
              if P.dma_count[q][r] > 0]

    n_instr = P.emit_all(nc, sems)

    with nc.Block() as block:
        @block.sync
        def _(sync):
            for name, val in finals:
                sync.wait_ge(sems[name], val)

    nc.finalize()
    return nc, n_instr



def prep_common(cfg, Wq, bq, Wk, bk, Wv, bv, Wo, bo,
                gqw, gsc, gqz, uqw, usc, uqz, dqw, dsc, dqz):
    T, D, H, I = cfg["T"], cfg["D"], cfg["H"], cfg["I"]
    KD, NST, PC, SJ = cfg["KD"], cfg["NST"], cfg["PC"], cfg["SJ"]
    JG, NC_DN, DNW = cfg["JG"], cfg["NC_DN"], cfg["DNW"]

    def headify(W):
        WT = np.ascontiguousarray(W.T)
        r = WT.reshape(KD, 128, H, 128)
        return np.ascontiguousarray(
            r.transpose(2, 1, 0, 3).reshape(H, 128, KD * 128))

    def pack_qw(qw, nst, pc, ktiles):
        r = qw.reshape(ktiles, 128, nst, pc)
        return np.ascontiguousarray(
            r.transpose(2, 1, 0, 3).reshape(nst, 128, ktiles * pc))

    def pack_rows(a, nst, w, groups):
        r = a.reshape(groups, nst, w)
        return np.ascontiguousarray(r.transpose(1, 0, 2))

    common = {
        "wq": headify(Wq), "wk": headify(Wk), "wv": headify(Wv),
        "wo": np.ascontiguousarray(Wo.T).reshape(KD, 128, D),
        "bq": np.ascontiguousarray(bq.reshape(1, D)),
        "bk": np.ascontiguousarray(bk.reshape(1, D)),
        "bvc": np.ascontiguousarray(bv.reshape(D, 1)),
        "bo": np.ascontiguousarray(bo.reshape(1, D)),
        "gqw": pack_qw(gqw, NST, PC, KD),
        "gsc": pack_rows(gsc, NST, SJ, KD),
        "gqz": pack_rows(gqz, NST, PC, KD),
        "uqw": pack_qw(uqw, NST, PC, KD),
        "usc": pack_rows(usc, NST, SJ, KD),
        "uqz": pack_rows(uqz, NST, PC, KD),
        "dqw": pack_qw(dqw, NC_DN, DNW // 8, JG),
        "dsc": pack_rows(dsc, NC_DN, DNW, JG),
        "dqz": pack_rows(dqz, NC_DN, DNW // 8, JG),
    }
    common.update(host_consts(cfg))
    return common


def kernel(L0_attn_x, L0_attn_Wq, L0_attn_bq, L0_attn_Wk, L0_attn_bk,
           L0_attn_Wv, L0_attn_bv, L0_attn_Wo, L0_attn_bo,
           L0_ffn_gate_qweight, L0_ffn_gate_scales, L0_ffn_gate_qzeros,
           L0_ffn_up_qweight, L0_ffn_up_scales, L0_ffn_up_qzeros,
           L0_ffn_down_qweight, L0_ffn_down_scales, L0_ffn_down_qzeros):
    x = np.asarray(L0_attn_x, dtype=np.float32)
    B, T, D = x.shape
    cfg = make_cfg(T=T, D=D, H=32, I=11008)
    common = prep_common(
        cfg,
        np.asarray(L0_attn_Wq, np.float32), np.asarray(L0_attn_bq, np.float32),
        np.asarray(L0_attn_Wk, np.float32), np.asarray(L0_attn_bk, np.float32),
        np.asarray(L0_attn_Wv, np.float32), np.asarray(L0_attn_bv, np.float32),
        np.asarray(L0_attn_Wo, np.float32), np.asarray(L0_attn_bo, np.float32),
        np.asarray(L0_ffn_gate_qweight, np.int32),
        np.asarray(L0_ffn_gate_scales, np.float32),
        np.asarray(L0_ffn_gate_qzeros, np.int32),
        np.asarray(L0_ffn_up_qweight, np.int32),
        np.asarray(L0_ffn_up_scales, np.float32),
        np.asarray(L0_ffn_up_qzeros, np.int32),
        np.asarray(L0_ffn_down_qweight, np.int32),
        np.asarray(L0_ffn_down_scales, np.float32),
        np.asarray(L0_ffn_down_qzeros, np.int32))
    in_maps = []
    for c in range(B):
        m = dict(common)
        m["x"] = np.ascontiguousarray(x[c])
        in_maps.append(m)
    nc, _ = build_program(cfg)
    res = run_bass_kernel_spmd(nc, in_maps, core_ids=list(range(B)))
    out = np.stack([np.asarray(r["out"]) for r in res.results], axis=0)
    return out.astype(np.float32)


# version 28
# speedup vs baseline: 1.0565x; 1.0061x over previous
import contextlib

import numpy as np
import ml_dtypes

import concourse.bass as bass
import concourse.mybir as mybir
import concourse.bass_utils as bass_utils
from concourse.bass_utils import run_bass_kernel_spmd



BF = ml_dtypes.bfloat16
F32 = mybir.dt.float32
BF16 = mybir.dt.bfloat16
I32 = mybir.dt.int32
AF = mybir.ActivationFunctionType
ALU = mybir.AluOpType
AX = mybir.AxisListType.X

NEG = -1.0e30
PI = [0, 2, 4, 6, 1, 3, 5, 7]


ENGINES = ("tensor", "vector", "scalar", "gpsimd", "sync")
DMA_SEM = {"sync": "dS", "gpsimd": "dG", "scalar": "dA"}
ENG_SEM = {"tensor": "sP", "vector": "sV", "scalar": "sA", "gpsimd": "sG",
           "sync": "sS"}
NLANE = 4


def sem_key(t):
    if t.dma:
        return DMA_SEM[t.eng] + str(t.lane)
    return ENG_SEM[t.eng]


class Tok:
    __slots__ = ("eng", "dma", "val", "lane")

    def __init__(self, eng, dma, val, lane=0):
        self.eng = eng
        self.dma = dma
        self.val = val
        self.lane = lane


class Buf:
    __slots__ = ("name", "writer", "readers")

    def __init__(self, name):
        self.name = name
        self.writer = None
        self.readers = []


class Step:
    __slots__ = ("eng", "emit", "deps", "dma", "n_dma", "tok", "tag")


class Prog:
    def __init__(self):
        self.steps = {e: [] for e in ENGINES}
        self.dma_count = {e: [0] * NLANE for e in DMA_SEM}
        self.dma_seq = {e: 0 for e in DMA_SEM}
        self.last_dma_tok = {e: [None] * NLANE for e in DMA_SEM}
        self.comp_count = {e: 0 for e in ENGINES}
        self.pending_barrier = None
        self._barrier_seen = set()

    def add(self, eng, emit, reads=(), writes=(), deps=(), dma=False,
            n_dma=1, tag=""):
        dep_toks = list(deps)
        for b in reads:
            if b.writer is not None:
                dep_toks.append(b.writer)
        for b in writes:
            if b.writer is not None:
                dep_toks.append(b.writer)
            dep_toks.extend(b.readers)
        if self.pending_barrier is not None:
            dep_toks.extend(self.pending_barrier)
        st = Step()
        st.eng, st.emit, st.tag = eng, emit, tag
        st.dma = dma
        st.n_dma = n_dma if dma else 0
        if dma:
            lane = self.dma_seq[eng] % NLANE
            self.dma_seq[eng] += 1
            prev = self.last_dma_tok[eng][lane]
            if prev is not None:
                dep_toks.append(prev)
            self.dma_count[eng][lane] += st.n_dma
            tok = Tok(eng, True, 16 * self.dma_count[eng][lane], lane)
            self.last_dma_tok[eng][lane] = tok
        else:
            self.comp_count[eng] += 1
            tok = Tok(eng, False, self.comp_count[eng])
        st.deps = dep_toks
        st.tok = tok
        self.steps[eng].append(st)
        for b in reads:
            b.readers.append(tok)
        for b in writes:
            b.writer = tok
            b.readers = []
        if self.pending_barrier is not None:
            self._barrier_seen.add(eng)
            if self._barrier_seen == set(ENGINES):
                self.pending_barrier = None
        return tok

    def barrier(self):
        toks = [self.steps[e][-1].tok for e in ENGINES if self.steps[e]]
        for q in DMA_SEM:
            for t in self.last_dma_tok[q]:
                if t is not None:
                    toks.append(t)
        self.pending_barrier = toks
        self._barrier_seen = set()

    def verify(self):
        pos = {e: 0 for e in ENGINES}
        sem = {}

        def ok(t):
            return sem.get(sem_key(t), 0) >= t.val

        total = sum(len(s) for s in self.steps.values())
        done = 0
        progress = True
        while progress:
            progress = False
            for e in ENGINES:
                while pos[e] < len(self.steps[e]):
                    st = self.steps[e][pos[e]]
                    if not all(ok(t) for t in st.deps):
                        break
                    key = sem_key(st.tok)
                    sem[key] = sem.get(key, 0) + (16 * st.n_dma if st.dma
                                                  else 1)
                    pos[e] += 1
                    done += 1
                    progress = True
        if done != total:
            stuck = []
            for e in ENGINES:
                if pos[e] < len(self.steps[e]):
                    st = self.steps[e][pos[e]]
                    bad = [(t.eng, t.dma, t.val) for t in st.deps
                           if not ok(t)]
                    stuck.append((e, pos[e], st.tag, bad[:3]))
            raise RuntimeError(f"schedule deadlock: {done}/{total}; "
                               f"stuck={stuck[:4]}")

    def emit_all(self, nc, sems):
        self.verify()
        n_instr = [0]

        def make_body(ename):
            def body(eng):
                watermark = {}
                for st in self.steps[ename]:
                    for t in st.deps:
                        key = sem_key(t)
                        if watermark.get(key, 0) >= t.val:
                            continue
                        eng.wait_ge(sems[key], t.val)
                        watermark[key] = t.val
                        n_instr[0] += 1
                    if st.dma:
                        st.emit(eng, sems[sem_key(st.tok)])
                        n_instr[0] += st.n_dma
                    else:
                        last = st.emit(eng)
                        assert last is not None, f"empty step {st.tag}"
                        last.then_inc(sems[ENG_SEM[ename]], 1)
                        n_instr[0] += 1
            return body

        with nc.Block() as block:
            block.tensor(make_body("tensor"))
            block.vector(make_body("vector"))
            block.scalar(make_body("scalar"))
            block.gpsimd(make_body("gpsimd"))
            block.sync(make_body("sync"))
        return n_instr[0]



def make_cfg(T=2048, D=4096, H=32, I=11008):
    HD = 128
    assert D == H * HD and T % 128 == 0 and I % 128 == 0
    cfg = dict(T=T, D=D, H=H, I=I, HD=HD)
    cfg["NT"] = T // 128
    cfg["KD"] = D // 128
    cfg["JG"] = I // 128
    cfg["CH"] = min(512, T)
    cfg["TPC"] = cfg["CH"] // 128
    cfg["NCH"] = T // cfg["CH"]
    cfg["TC5"] = min(512, T)
    cfg["NT5"] = T // cfg["TC5"]
    cfg["JH"] = min(2048, D)
    cfg["NJH"] = D // cfg["JH"]
    cfg["PC"] = 16
    cfg["SJ"] = 8 * cfg["PC"]
    cfg["NST"] = I // cfg["SJ"]
    cfg["DNW"] = min(512, D)
    cfg["NC_DN"] = D // cfg["DNW"]
    cfg["TPR"] = min(4, cfg["NT"])
    cfg["NTR"] = cfg["NT"] // cfg["TPR"]
    return cfg


def host_consts(cfg):
    tri = np.zeros((128, 128), dtype=BF)
    for p in range(128):
        tri[p, p + 1:] = BF(NEG)
    ident = np.eye(128, dtype=BF)
    ones = np.ones((1, 512), dtype=BF)
    KD, JG = cfg["KD"], cfg["JG"]
    hot32 = np.zeros((KD, KD * 128), dtype=BF)
    for g in range(KD):
        hot32[g, g * 128:(g + 1) * 128] = 1
    hotJG = np.zeros((JG, JG * 128), dtype=BF)
    for g in range(JG):
        hotJG[g, g * 128:(g + 1) * 128] = 1
    conehot = np.zeros((128, KD * KD), dtype=BF)
    for g in range(KD):
        conehot[:, g * KD + g] = 1
    return dict(cident=ident, cones=ones, ctri=tri, chot32=hot32,
                chotJG=hotJG, cconehot=conehot)



def build_program(cfg):
    T, D, H, I, HD = cfg["T"], cfg["D"], cfg["H"], cfg["I"], cfg["HD"]
    NT, KD, JG = cfg["NT"], cfg["KD"], cfg["JG"]
    CH, TPC, NCH = cfg["CH"], cfg["TPC"], cfg["NCH"]
    TC5, NT5 = cfg["TC5"], cfg["NT5"]
    JH, NJH = cfg["JH"], cfg["NJH"]
    PC, SJ, NST = cfg["PC"], cfg["SJ"], cfg["NST"]
    DNW, NC_DN, TPR, NTR = cfg["DNW"], cfg["NC_DN"], cfg["TPR"], cfg["NTR"]

    nc = bass.Bass("TRN2", target_bir_lowering=False)

    d = {}
    d["x"] = nc.dram_tensor("x", [T, D], F32, kind="ExternalInput").ap()
    for w in ("wq", "wk", "wv"):
        d[w] = nc.dram_tensor(w, [H, 128, KD * 128], F32,
                              kind="ExternalInput").ap()
    d["wo"] = nc.dram_tensor("wo", [KD, 128, D], F32,
                             kind="ExternalInput").ap()
    d["bq"] = nc.dram_tensor("bq", [1, D], F32, kind="ExternalInput").ap()
    d["bk"] = nc.dram_tensor("bk", [1, D], F32, kind="ExternalInput").ap()
    d["bvc"] = nc.dram_tensor("bvc", [D, 1], F32, kind="ExternalInput").ap()
    d["bo"] = nc.dram_tensor("bo", [1, D], F32, kind="ExternalInput").ap()
    for m in ("g", "u"):
        d[m + "qw"] = nc.dram_tensor(m + "qw", [NST, 128, KD * PC], I32,
                                     kind="ExternalInput").ap()
        d[m + "sc"] = nc.dram_tensor(m + "sc", [NST, KD, SJ], F32,
                                     kind="ExternalInput").ap()
        d[m + "qz"] = nc.dram_tensor(m + "qz", [NST, KD, PC], I32,
                                     kind="ExternalInput").ap()
    d["dqw"] = nc.dram_tensor("dqw", [NC_DN, 128, JG * (DNW // 8)], I32,
                              kind="ExternalInput").ap()
    d["dsc"] = nc.dram_tensor("dsc", [NC_DN, JG, DNW], F32,
                              kind="ExternalInput").ap()
    d["dqz"] = nc.dram_tensor("dqz", [NC_DN, JG, DNW // 8], I32,
                              kind="ExternalInput").ap()
    d["cident"] = nc.dram_tensor("cident", [128, 128], BF16,
                                 kind="ExternalInput").ap()
    d["cones"] = nc.dram_tensor("cones", [1, 512], BF16,
                                kind="ExternalInput").ap()
    d["ctri"] = nc.dram_tensor("ctri", [128, 128], BF16,
                               kind="ExternalInput").ap()
    d["chot32"] = nc.dram_tensor("chot32", [KD, KD * 128], BF16,
                                 kind="ExternalInput").ap()
    d["cconehot"] = nc.dram_tensor("cconehot", [128, KD * KD], BF16,
                                   kind="ExternalInput").ap()
    d["chotJG"] = nc.dram_tensor("chotJG", [JG, JG * 128], BF16,
                                 kind="ExternalInput").ap()
    d["attnT"] = nc.dram_tensor("attnT", [H, 128, T], BF16).ap()
    d["x1"] = nc.dram_tensor("x1", [T, D], F32).ap()
    d["hT"] = nc.dram_tensor("hT", [JG, 128, T], BF16).ap()
    d["out"] = nc.dram_tensor("out", [T, D], F32, kind="ExternalOutput").ap()

    sems = {}
    for name in ENG_SEM.values():
        sems[name] = nc.alloc_semaphore(name)
    for name in DMA_SEM.values():
        for r in range(NLANE):
            sems[name + str(r)] = nc.alloc_semaphore(name + str(r))

    P = Prog()

    ident16 = nc.alloc_sbuf_tensor("ident16", [128, 128], BF16)
    ones16 = nc.alloc_sbuf_tensor("ones16", [1, 512], BF16)
    tri16 = nc.alloc_sbuf_tensor("tri16", [128, 128], BF16)
    LNW = max(T, JH)
    outer_es = contextlib.ExitStack()
    lnT = outer_es.enter_context(
        nc.sbuf_tensor("lnT", [128, KD * LNW], BF16))

    b_ident = Buf("ident16"); b_ones = Buf("ones16"); b_tri = Buf("tri16")
    b_lnT = Buf("lnT")

    def lnT3():
        return lnT[:].rearrange("p (k w) -> p k w", k=KD)

    def dma_step(eng_name, pairs, reads=(), writes=(), deps=(), tag=""):
        def emit(eng, dsem):
            for dst, src in pairs:
                eng.dma_start(dst, src).then_inc(dsem, 16)
        return P.add(eng_name, emit, reads=reads, writes=writes, deps=deps,
                     dma=True, n_dma=len(pairs), tag=tag)

    dma_step("sync", [(ident16[:], d["cident"]), (ones16[:], d["cones"]),
                      (tri16[:], d["ctri"])],
             writes=[b_ident, b_ones, b_tri], tag="consts")

    stack = contextlib.ExitStack()

    def emit_layernorm(src_ap, ph):
        with contextlib.ExitStack() as es:
            xst = [es.enter_context(
                nc.sbuf_tensor(f"xst{ph}_{i}", [128, D], F32))
                for i in range(3)]
            lnd = es.enter_context(nc.sbuf_tensor(f"lnd{ph}", [128, D], BF16))
            ln16 = [es.enter_context(
                nc.sbuf_tensor(f"ln16{ph}_{i}", [128, D], BF16))
                for i in range(2)]
            colsr = [es.enter_context(
                nc.sbuf_tensor(f"lncol{ph}_{i}", [128, 8], F32))
                for i in range(4)]
            p_ln = [es.enter_context(
                nc.psum_tensor(f"pln{ph}_{i}", [128, 2048], BF16))
                for i in range(2)]
            b_xst = [Buf("xst0"), Buf("xst1"), Buf("xst2")]
            b_lnd = Buf("lnd")
            b_ln16 = [Buf("ln16_0"), Buf("ln16_1")]
            b_colsr = [Buf(f"lncol{i}") for i in range(4)]
            b_pln = [Buf("pln0"), Buf("pln1")]
            HKT = 2048 // 128

            for tc in range(NT):
                par = tc % 3
                cols, b_cols = colsr[tc % 4], b_colsr[tc % 4]
                x_t, bx = xst[par], b_xst[par]
                dma_step("scalar",
                         [(x_t[:], src_ap[tc * 128:(tc + 1) * 128, :])],
                         writes=[bx], tag=f"ln{ph}.x{tc}")
                P.add("vector",
                      lambda e, x_t=x_t, cols=cols: e.reduce_sum(
                          cols[:, 0:1], x_t[:], axis=AX),
                      reads=[bx], writes=[b_cols], tag=f"ln{ph}.sum{tc}")
                P.add("scalar",
                      lambda e, x_t=x_t, cols=cols: e.activation(
                          lnd[:], x_t[:], AF.Square,
                          accum_out=cols[:, 1:2]),
                      reads=[bx], writes=[b_lnd, b_cols],
                      tag=f"ln{ph}.sq{tc}")
                P.add("vector",
                      lambda e, cols=cols: e.tensor_scalar(
                          cols[:, 2:3], cols[:, 0:1],
                          1.0 / D, 0.0, ALU.mult, ALU.add),
                      reads=[b_cols], writes=[b_cols], tag=f"ln{ph}.mean{tc}")
                P.add("vector",
                      lambda e, cols=cols: e.tensor_tensor(
                          cols[:, 3:4], cols[:, 2:3],
                          cols[:, 2:3], ALU.mult),
                      reads=[b_cols], writes=[b_cols], tag=f"ln{ph}.m2{tc}")
                P.add("vector",
                      lambda e, cols=cols: e.scalar_tensor_tensor(
                          cols[:, 4:5], cols[:, 1:2], 1.0 / D, cols[:, 3:4],
                          ALU.mult, ALU.subtract),
                      reads=[b_cols], writes=[b_cols], tag=f"ln{ph}.var{tc}")
                P.add("vector",
                      lambda e, cols=cols: e.tensor_scalar(
                          cols[:, 4:5], cols[:, 4:5],
                          1e-5, 1.0, ALU.add, ALU.mult),
                      reads=[b_cols], writes=[b_cols], tag=f"ln{ph}.eps{tc}")
                P.add("scalar",
                      lambda e, cols=cols: e.activation(
                          cols[:, 5:6], cols[:, 4:5], AF.Ln),
                      reads=[b_cols], writes=[b_cols], tag=f"ln{ph}.lnv{tc}")
                P.add("scalar",
                      lambda e, cols=cols: e.activation(
                          cols[:, 6:7], cols[:, 5:6], AF.Exp, scale=-0.5),
                      reads=[b_cols], writes=[b_cols], tag=f"ln{ph}.rstd{tc}")
                P.add("vector",
                      lambda e, cols=cols: e.scalar_tensor_tensor(
                          cols[:, 7:8], cols[:, 2:3], -1.0, cols[:, 6:7],
                          ALU.mult, ALU.mult),
                      reads=[b_cols], writes=[b_cols], tag=f"ln{ph}.nb{tc}")
                l16, bl16 = ln16[tc % 2], b_ln16[tc % 2]
                P.add("vector",
                      lambda e, x_t=x_t, l16=l16, cols=cols: e.tensor_scalar(
                          l16[:], x_t[:], cols[:, 6:7], cols[:, 7:8],
                          ALU.mult, ALU.add),
                      reads=[bx, b_cols], writes=[bl16],
                      tag=f"ln{ph}.norm{tc}")
                nbat = (KD + HKT - 1) // HKT
                for bi in range(nbat):
                    k0 = bi * HKT
                    k1 = min(KD, k0 + HKT)
                    pp, bpp = p_ln[bi % 2], b_pln[bi % 2]

                    def em_tr(e, l16=l16, k0=k0, k1=k1, pp=pp):
                        last = None
                        for k in range(k0, k1):
                            last = e.transpose(
                                pp[:, (k - k0) * 128:(k - k0 + 1) * 128],
                                l16[:, k * 128:(k + 1) * 128], ident16[:])
                        return last
                    P.add("tensor", em_tr, reads=[bl16, b_ident],
                          writes=[bpp], tag=f"ln{ph}.tr{tc}.{bi}")

                    def em_cp(e, k0=k0, k1=k1, pp=pp, tc=tc):
                        dst = lnT3()[:, k0:k1, tc * 128:(tc + 1) * 128]
                        return e.tensor_copy(dst, pp[:, 0:(k1 - k0) * 128])
                    P.add("vector", em_cp, reads=[bpp], writes=[b_lnT],
                          tag=f"ln{ph}.cp{tc}.{bi}")

    emit_layernorm(d["x"], 1)
    P.barrier()

    with contextlib.ExitStack() as es:
        w16 = {m: es.enter_context(
            nc.sbuf_tensor(f"w16{m}", [128, KD * 128], BF16))
            for m in ("q", "k", "v")}
        b_w16 = {m: Buf(f"w16{m}") for m in ("q", "k", "v")}
        bh16 = [es.enter_context(
            nc.sbuf_tensor(f"bh16_{i}", [1, 256], BF16)) for i in range(2)]
        b_bh = [Buf("bh0"), Buf("bh1")]
        bv_col = [es.enter_context(
            nc.sbuf_tensor(f"bv_col{i}", [128, 1], F32)) for i in range(2)]
        b_bvc = [Buf("bv0"), Buf("bv1")]
        qTr = [es.enter_context(
            nc.sbuf_tensor(f"qT{i}", [128, T], BF16)) for i in range(2)]
        kTr = [es.enter_context(
            nc.sbuf_tensor(f"kT{i}", [128, T], BF16)) for i in range(2)]
        a16v = [es.enter_context(
            nc.sbuf_tensor(f"a16v{i}", [128, T], BF16)) for i in range(2)]
        vNr = [es.enter_context(
            nc.sbuf_tensor(f"vN{i}", [128, NT * 128], BF16))
            for i in range(2)]
        b_qT = [Buf("qT0"), Buf("qT1")]
        b_kT = [Buf("kT0"), Buf("kT1")]
        b_vN = [Buf("vN0"), Buf("vN1")]
        b_a16v = [Buf("a16v0"), Buf("a16v1")]
        acol = [es.enter_context(
            nc.sbuf_tensor(f"acol{i}", [128, 8], F32)) for i in range(2)]
        diag16 = [es.enter_context(
            nc.sbuf_tensor(f"diag16_{i}", [128, 128], BF16))
            for i in range(2)]
        b_acol = [Buf("ac0"), Buf("ac1")]
        b_diag = [Buf("dg0"), Buf("dg1")]
        attTb = es.enter_context(
            nc.sbuf_tensor("attTb", [128, NT * CH], BF16))
        b_attTb = Buf("attTb")
        oT = es.enter_context(nc.sbuf_tensor("oT", [128, T], BF16))
        b_oT = Buf("oT")
        p_qkv = es.enter_context(nc.psum_tensor("p_qkv", [128, 512], F32))
        b_pqkv = Buf("p_qkv")
        p_sc = es.enter_context(
            nc.psum_tensor("p_sc", [128, min(T, 2048)], F32))
        b_psc = [Buf("p_sc_lo"), Buf("p_sc_hi")]
        p_tr = es.enter_context(nc.psum_tensor("p_tr", [128, 1024], F32))
        b_ptr = [Buf("p_tr0"), Buf("p_tr1")]
        p_av = es.enter_context(nc.psum_tensor("p_av", [128, 512], F32))
        b_pav = Buf("p_av")

        def attTb3():
            return attTb[:].rearrange("p (t c) -> p t c", t=NT)

        def fetch_head_w(h):
            for m, wname in (("q", "wq"), ("k", "wk"), ("v", "wv")):
                dma_step("gpsimd", [(w16[m][:], d[wname][h])],
                         writes=[b_w16[m]], tag=f"h{h}.w{m}")
            hp = h % 2
            dma_step("gpsimd",
                     [(bh16[hp][:, 0:128], d["bq"][:, h * 128:(h + 1) * 128]),
                      (bh16[hp][:, 128:256],
                       d["bk"][:, h * 128:(h + 1) * 128])],
                     writes=[b_bh[hp]], tag=f"h{h}.b")
            dma_step("sync",
                     [(bv_col[hp][:], d["bvc"][h * 128:(h + 1) * 128, :])],
                     writes=[b_bvc[hp]], tag=f"h{h}.bv")

        def qkv_units(h):
            hp = h % 2
            units = []
            for m, dstT, bdst, bias_sl, scale in (
                    ("q", qTr[hp], b_qT[hp], (0, 128),
                     1.0 / float(np.sqrt(HD))),
                    ("k", kTr[hp], b_kT[hp], (128, 256), 1.0),
                    ("v", a16v[hp], b_a16v[hp], None, 1.0)):
                for tch in range(NT5):
                    def unit(m=m, dstT=dstT, bdst=bdst, bias_sl=bias_sl,
                             scale=scale, tch=tch, h=h, hp=hp):
                        def em_mm(e):
                            last = None
                            for k in range(KD):
                                last = e.matmul(
                                    p_qkv[:, 0:TC5],
                                    w16[m][:, k * 128:(k + 1) * 128],
                                    lnT3()[:, k, tch * TC5:(tch + 1) * TC5],
                                    start=(k == 0),
                                    stop=(bias_sl is None and k == KD - 1))
                            if bias_sl is not None:
                                last = e.matmul(
                                    p_qkv[:, 0:TC5],
                                    bh16[hp][:, bias_sl[0]:bias_sl[1]],
                                    ones16[:, 0:TC5],
                                    start=False, stop=True)
                            return last
                        reads = [b_w16[m], b_lnT, b_ones]
                        if bias_sl is not None:
                            reads.append(b_bh[hp])
                        P.add("tensor", em_mm, reads=reads, writes=[b_pqkv],
                              tag=f"h{h}.{m}mm{tch}")
                        P.add("vector",
                              lambda e, dstT=dstT, tch=tch, scale=scale:
                              e.tensor_scalar(
                                  dstT[:, tch * TC5:(tch + 1) * TC5],
                                  p_qkv[:, 0:TC5], scale, 0.0,
                                  ALU.mult, ALU.add),
                              reads=[b_pqkv], writes=[bdst],
                              tag=f"h{h}.{m}ev{tch}")
                    units.append(unit)
            for bi in range((NT + 3) // 4):
                def unit(bi=bi, h=h, hp=hp):
                    i0v, i1v = bi * 4, min(NT, bi * 4 + 4)
                    half = bi % 2
                    bptr = b_ptr[half]

                    def em_vtr(e):
                        last = None
                        for i in range(i0v, i1v):
                            last = e.matmul(
                                p_tr[:, half * 512 + (i - i0v) * 128:
                                     half * 512 + (i - i0v + 1) * 128],
                                a16v[hp][:, i * 128:(i + 1) * 128],
                                ident16[:], start=True, stop=True)
                        return last
                    P.add("tensor", em_vtr, reads=[b_a16v[hp], b_ident],
                          writes=[bptr], tag=f"h{h}.vtr{bi}")

                    def em_vcp(e):
                        return e.tensor_copy(
                            vNr[hp][:, i0v * 128:i1v * 128],
                            p_tr[:, half * 512:half * 512 +
                                 (i1v - i0v) * 128])
                    P.add("vector", em_vcp, reads=[bptr],
                          writes=[b_vN[hp]], tag=f"h{h}.vcp{bi}")
                units.append(unit)
            return units

        fetch_head_w(0)
        for u in qkv_units(0):
            u()
        for h in range(H):
            hp = h % 2
            qT, kT, vN = qTr[hp], kTr[hp], vNr[hp]
            bqT, bkT, bvN = b_qT[hp], b_kT[hp], b_vN[hp]
            att16 = [a16v[hp]]
            b_a16 = [b_a16v[hp]]
            if h + 1 < H:
                fetch_head_w(h + 1)
                nxt = qkv_units(h + 1)
            else:
                nxt = []
            n_nxt0 = len(nxt)
            tidx = 0
            for c in range(NCH):
                pend = [None]
                for il in range(TPC):
                    ig = c * TPC + il
                    W = (ig + 1) * 128
                    par = ig % 2
                    a16s, ac, dg = att16[0], acol[par], diag16[par]
                    ba16, bac, bdg = b_a16[0], b_acol[par], b_diag[par]
                    HSC = min(T, 2048) // 2
                    if W <= HSC:
                        base = HSC * (ig % 2)
                        bpsc = [b_psc[ig % 2]]
                    else:
                        base = 0
                        bpsc = b_psc
                    chunks = [(tkc, min(512, W - tkc))
                              for tkc in range(0, W, 512)]
                    nch = len(chunks)
                    for ci, (tkc, wdt) in enumerate(chunks):
                        hb = ([b_psc[(base + tkc) // HSC]]
                              if wdt + tkc + base <= HSC or base + tkc >= HSC
                              else bpsc)

                        def em_scc(e, ig=ig, tkc=tkc, wdt=wdt, base=base,
                                   qT=qT, kT=kT):
                            diag_in = tkc <= ig * 128 < tkc + wdt
                            last = e.matmul(
                                p_sc[:, base + tkc:base + tkc + wdt],
                                qT[:, ig * 128:(ig + 1) * 128],
                                kT[:, tkc:tkc + wdt],
                                start=True, stop=not diag_in)
                            if diag_in:
                                last = e.matmul(
                                    p_sc[:, base + ig * 128:
                                         base + (ig + 1) * 128],
                                    ident16[:], tri16[:],
                                    start=False, stop=True)
                            return last
                        P.add("tensor", em_scc,
                              reads=[bqT, bkT, b_ident, b_tri],
                              writes=hb, tag=f"h{h}.sc{ig}.{ci}")
                        if nch == 1:
                            P.add("vector",
                                  lambda e, W=W, ac=ac, base=base:
                                  e.reduce_max(ac[:, 0:1],
                                               p_sc[:, base:base + W],
                                               axis=AX, negate=True),
                                  reads=hb, writes=[bac],
                                  tag=f"h{h}.mx{ig}")
                        else:
                            P.add("vector",
                                  lambda e, ci=ci, tkc=tkc, wdt=wdt,
                                  ac=ac, base=base:
                                  e.reduce_max(ac[:, 4 + ci:5 + ci],
                                               p_sc[:, base + tkc:
                                                    base + tkc + wdt],
                                               axis=AX),
                                  reads=hb, writes=[bac],
                                  tag=f"h{h}.mxc{ig}.{ci}")
                        if ci == 0 and pend[0] is not None:
                            pend[0]()
                            pend[0] = None
                            tidx += 1
                            if tidx >= 2:
                                share = (n_nxt0 * (tidx - 1)) // \
                                    max(1, NT - 2)
                                while nxt and n_nxt0 - len(nxt) < share:
                                    nxt.pop(0)()
                    if nch > 1:
                        P.add("vector",
                              lambda e, ac=ac, nch=nch: e.reduce_max(
                                  ac[:, 0:1], ac[:, 4:4 + nch], axis=AX,
                                  negate=True),
                              reads=[bac], writes=[bac],
                              tag=f"h{h}.mx{ig}")
                    if W <= HSC:
                        P.add("scalar",
                              lambda e, W=W, a16s=a16s, ac=ac, base=base:
                              e.activation(
                                  a16s[:, 0:W], p_sc[:, base:base + W],
                                  AF.Exp, bias=ac[:, 0:1],
                                  accum_out=ac[:, 1:2]),
                              reads=bpsc + [bac], writes=[ba16, bac],
                              tag=f"h{h}.ex{ig}")
                        P.add("vector",
                              lambda e, ac=ac: e.reciprocal(ac[:, 2:3],
                                                            ac[:, 1:2]),
                              reads=[bac], writes=[bac],
                              tag=f"h{h}.rc{ig}")
                    else:
                        P.add("scalar",
                              lambda e, a16s=a16s, ac=ac, HSC=HSC:
                              e.activation(
                                  a16s[:, 0:HSC], p_sc[:, 0:HSC],
                                  AF.Exp, bias=ac[:, 0:1],
                                  accum_out=ac[:, 1:2]),
                              reads=[b_psc[0], bac], writes=[ba16, bac],
                              tag=f"h{h}.exlo{ig}")
                        P.add("scalar",
                              lambda e, W=W, a16s=a16s, ac=ac, HSC=HSC:
                              e.activation(
                                  a16s[:, HSC:W], p_sc[:, HSC:W],
                                  AF.Exp, bias=ac[:, 0:1],
                                  accum_out=ac[:, 3:4]),
                              reads=[b_psc[1], bac], writes=[ba16, bac],
                              tag=f"h{h}.exhi{ig}")
                        P.add("vector",
                              lambda e, ac=ac: e.scalar_tensor_tensor(
                                  ac[:, 1:2], ac[:, 1:2], 1.0, ac[:, 3:4],
                                  ALU.mult, ALU.add),
                              reads=[bac], writes=[bac],
                              tag=f"h{h}.zs{ig}")
                        P.add("vector",
                              lambda e, ac=ac: e.reciprocal(ac[:, 2:3],
                                                            ac[:, 1:2]),
                              reads=[bac], writes=[bac],
                              tag=f"h{h}.rc{ig}")
                    P.add("vector",
                          lambda e, ac=ac, dg=dg: e.tensor_scalar(
                              dg[:], ident16[:], ac[:, 2:3], 0.0,
                              ALU.mult, ALU.add),
                          reads=[bac, b_ident], writes=[bdg],
                          tag=f"h{h}.dg{ig}")

                    def make_flush(ig=ig, il=il, c=c, a16s=a16s, dg=dg,
                                   ba16=ba16, bdg=bdg):
                        def flush():
                            ntile = ig + 1
                            for bi in range((ntile + 3) // 4):
                                k0, k1 = bi * 4, min(ntile, bi * 4 + 4)
                                half = bi % 2
                                bptr = b_ptr[half]

                                def em_atr(e, a16s=a16s, dg=dg, k0=k0,
                                           k1=k1, half=half):
                                    last = None
                                    for kt in range(k0, k1):
                                        last = e.matmul(
                                            p_tr[:, half * 512 +
                                                 (kt - k0) * 128:
                                                 half * 512 +
                                                 (kt - k0 + 1) * 128],
                                            a16s[:, kt * 128:(kt + 1) * 128],
                                            dg[:], start=True, stop=True)
                                    return last
                                P.add("tensor", em_atr,
                                      reads=[ba16, bdg], writes=[bptr],
                                      tag=f"h{h}.atr{ig}.{bi}")

                                def em_acp(e, k0=k0, k1=k1, half=half,
                                           il=il):
                                    srcp = p_tr[:, half * 512:
                                                half * 512 +
                                                (k1 - k0) * 128]
                                    dst = attTb3()[:, k0:k1,
                                                   il * 128:(il + 1) * 128]
                                    return e.copy(dst, srcp)
                                P.add("scalar", em_acp, reads=[bptr],
                                      writes=[b_attTb],
                                      tag=f"h{h}.acp{ig}.{bi}")
                            ntail = (c + 1) * TPC - ntile
                            if ntail > 0:
                                def em_z(e, ntile=ntile, ntail=ntail,
                                         il=il):
                                    dst = attTb3()[:, ntile:ntile + ntail,
                                                   il * 128:(il + 1) * 128]
                                    return e.memset(dst, 0.0)
                                P.add("gpsimd", em_z, writes=[b_attTb],
                                      tag=f"h{h}.zz{ig}")
                        return flush
                    pend[0] = make_flush()
                if pend[0] is not None:
                    pend[0]()
                    pend[0] = None

                nkt = (c + 1) * TPC

                def em_av(e, c=c, nkt=nkt, vN=vN):
                    last = None
                    for kt in range(nkt):
                        last = e.matmul(p_av[:, 0:CH],
                                        vN[:, kt * 128:(kt + 1) * 128],
                                        attTb3()[:, kt, 0:CH],
                                        start=(kt == 0), stop=(kt == nkt - 1))
                    return last
                P.add("tensor", em_av, reads=[bvN, b_attTb], writes=[b_pav],
                      tag=f"h{h}.av{c}")

                def em_oev(e, c=c, h=h):
                    return e.activation(oT[:, c * CH:(c + 1) * CH],
                                        p_av[:, 0:CH], AF.Identity,
                                        bias=bv_col[h % 2][:], scale=1.0)
                P.add("scalar", em_oev, reads=[b_pav, b_bvc[h % 2]],
                      writes=[b_oT], tag=f"h{h}.oev{c}")

            for u in nxt:
                u()
            dma_step("sync", [(d["attnT"][h], oT[:])], reads=[b_oT],
                     tag=f"h{h}.osp")

        P.barrier()

    with contextlib.ExitStack() as es:
        at_sb = [es.enter_context(
            nc.sbuf_tensor(f"at_sb{i}", [128, KD * 128], BF16))
            for i in range(4)]
        b_at = [Buf(f"at{i}") for i in range(4)]
        xo_sb = [es.enter_context(
            nc.sbuf_tensor(f"xo_sb{i}", [128, JH], F32)) for i in range(2)]
        b_xo = [Buf("xo0"), Buf("xo1")]
        x1_sb = [es.enter_context(
            nc.sbuf_tensor(f"x1_sb{i}", [128, JH], F32)) for i in range(2)]
        b_x1 = [Buf("x1o0"), Buf("x1o1")]
        bo16 = es.enter_context(nc.sbuf_tensor("bo16", [1, D], BF16))
        b_bo = Buf("bo16")
        p_o = [es.enter_context(
            nc.psum_tensor(f"p_o{i}", [128, min(JH, 2048)], F32))
            for i in range(2)]
        b_po = [Buf("po0"), Buf("po1")]

        dma_step("gpsimd", [(bo16[:], d["bo"])], writes=[b_bo], tag="bo")

        OPROJ_PING = 2 * JH <= LNW
        if OPROJ_PING:
            b_woq = [[Buf(f"woq{q}_{i}") for i in range(4)]
                     for q in range(2)]
            QOF = JH
        else:
            _bw = [Buf("woq_0")] * 4
            b_woq = [_bw, _bw]
            QOF = 0

        for jh in range(NJH):
            qp = jh % 2
            ksp = max(1, KD // 4)
            for si in range(0, KD, ksp):
                pairs = [(lnT3()[:, k, qp * QOF:qp * QOF + JH],
                          d["wo"][k, :, jh * JH:(jh + 1) * JH])
                         for k in range(si, min(KD, si + ksp))]
                dma_step("gpsimd", pairs, writes=[b_woq[qp][si // ksp]],
                         tag=f"op.w{jh}.{si}")
            for tc in range(NT):
                par = tc % 4
                a_s, b_as = at_sb[par], b_at[par]
                x_s, b_xs = xo_sb[tc % 2], b_xo[tc % 2]
                x1s, b_x1s = x1_sb[tc % 2], b_x1[tc % 2]
                po, b_pos = p_o[tc % 2], b_po[tc % 2]
                npart = (KD + 7) // 8
                pairs = []
                a3 = a_s[:].rearrange("p (t j) -> p t j", t=KD)
                for gi in range(npart):
                    k0, k1 = gi * 8, min(KD, gi * 8 + 8)
                    src = d["attnT"][k0:k1, :, tc * 128:(tc + 1) * 128]
                    pairs.append((a3[:, k0:k1, :], src.transpose([1, 0, 2])))
                dma_step("sync", pairs, writes=[b_as], tag=f"op.at{jh}.{tc}")
                dma_step("scalar",
                         [(x_s[:], d["x"][tc * 128:(tc + 1) * 128,
                                          jh * JH:(jh + 1) * JH])],
                         writes=[b_xs], tag=f"op.x{jh}.{tc}")

                def em_omm(e, a_s=a_s, po=po, jh=jh, qp=qp):
                    last = None
                    for k in range(KD):
                        for jc in range(0, JH, 512):
                            wdt = min(512, JH - jc)
                            last = e.matmul(
                                po[:, jc:jc + wdt],
                                a_s[:, k * 128:(k + 1) * 128],
                                lnT3()[:, k, qp * QOF + jc:
                                       qp * QOF + jc + wdt],
                                start=(k == 0), stop=False)
                    for jc in range(0, JH, 512):
                        wdt = min(512, JH - jc)
                        last = e.matmul(
                            po[:, jc:jc + wdt], ones16[:, 0:128],
                            bo16[:, jh * JH + jc:jh * JH + jc + wdt],
                            start=False, stop=True)
                    return last
                P.add("tensor", em_omm,
                      reads=[b_as, b_ones, b_bo] + list(b_woq[qp]),
                      writes=[b_pos], tag=f"op.mm{jh}.{tc}")

                def em_oadd(e, x1s=x1s, po=po, x_s=x_s):
                    return e.tensor_tensor(x1s[:], po[:, 0:JH], x_s[:],
                                           ALU.add)
                P.add("vector", em_oadd, reads=[b_pos, b_xs], writes=[b_x1s],
                      tag=f"op.add{jh}.{tc}")
                dma_step("sync",
                         [(d["x1"][tc * 128:(tc + 1) * 128,
                                   jh * JH:(jh + 1) * JH], x1s[:])],
                         reads=[b_x1s], tag=f"op.st{jh}.{tc}")
        P.barrier()

    emit_layernorm(d["x1"], 2)
    P.barrier()

    with contextlib.ExitStack() as es:
        hot32 = es.enter_context(
            nc.sbuf_tensor("hot32", [KD, KD * 128], BF16))
        b_h32 = Buf("hot32")
        conehot = es.enter_context(
            nc.sbuf_tensor("conehot", [128, KD * KD], BF16))
        b_coh = Buf("conehot")
        dma_step("sync", [(hot32[:], d["chot32"]),
                          (conehot[:], d["cconehot"])],
                 writes=[b_h32, b_coh], tag="hots")
        xgT = es.enter_context(nc.sbuf_tensor("xgT", [KD, T], BF16))
        b_xgT = Buf("xgT")
        wstb = [es.enter_context(
            nc.sbuf_tensor(f"wst{i}", [128, KD * SJ], BF16))
            for i in range(2)]
        b_wst = [Buf("wst0"), Buf("wst1")]
        KH = KD // 2 if KD % 2 == 0 else KD
        nibgu = es.enter_context(
            nc.sbuf_tensor("nibgu", [128, KH * SJ], I32))
        b_nib = Buf("nibgu")
        pk_sb, b_pk = {}, {}
        ssch, nzs, znib, qzst, b_ss, b_nzs, b_zs, b_qz = ({}, {}, {}, {},
                                                          {}, {}, {}, {})
        for m in ("g", "u"):
            pk_sb[m] = [es.enter_context(
                nc.sbuf_tensor(f"pk{m}{i}", [128, KD * PC], I32))
                for i in range(2)]
            ssch[m] = [es.enter_context(
                nc.sbuf_tensor(f"ssch{m}{i}", [KD, SJ], BF16))
                for i in range(2)]
            nzs[m] = [es.enter_context(
                nc.sbuf_tensor(f"nzs{m}{i}", [KD, SJ], BF16))
                for i in range(2)]
            znib[m] = [es.enter_context(
                nc.sbuf_tensor(f"znib{m}{i}", [KD, SJ], I32))
                for i in range(2)]
            qzst[m] = [es.enter_context(
                nc.sbuf_tensor(f"qzst{m}{i}", [KD, PC], I32))
                for i in range(2)]
            b_pk[m] = [Buf(f"pk{m}{i}") for i in range(2)]
            b_ss[m] = [Buf(f"ss{m}{i}") for i in range(2)]
            b_nzs[m] = [Buf(f"nzs{m}{i}") for i in range(2)]
            b_zs[m] = [Buf(f"zs{m}{i}") for i in range(2)]
            b_qz[m] = [Buf(f"qz{m}{i}") for i in range(2)]
        silu_sb = es.enter_context(nc.sbuf_tensor("silu_sb", [128, T], BF16))
        b_silu = Buf("silu")
        hTb = [es.enter_context(
            nc.sbuf_tensor(f"hTb{i}", [128, T], BF16)) for i in range(2)]
        b_hTb = [Buf("hTb0"), Buf("hTb1")]
        p_gu = {m: es.enter_context(
            nc.psum_tensor(f"p_{m}", [128, 1024], F32)) for m in ("g", "u")}
        b_pgu = {m: Buf(f"p_{m}") for m in ("g", "u")}
        p_s = [es.enter_context(
            nc.psum_tensor(f"p_s{i}", [128, 512], F32)) for i in range(4)]
        b_ps = [Buf(f"ps{i}") for i in range(4)]

        state = {"bc": 0}

        NT5X = (T + 511) // 512
        for tch in range(NT5X):
            w5 = min(512, T - tch * 512)

            def em_xg(e, tch=tch, w5=w5):
                last = None
                for g in range(KD):
                    last = e.matmul(
                        p_s[tch % 4][0:KD, 0:w5],
                        conehot[:, g * KD:(g + 1) * KD],
                        lnT3()[:, g, tch * 512:tch * 512 + w5],
                        start=(g == 0), stop=(g == KD - 1))
                return last
            P.add("tensor", em_xg, reads=[b_coh, b_lnT],
                  writes=[b_ps[tch % 4]], tag=f"xg.mm{tch}")
            P.add("vector",
                  lambda e, tch=tch, w5=w5: e.tensor_copy(
                      xgT[0:KD, tch * 512:tch * 512 + w5],
                      p_s[tch % 4][0:KD, 0:w5]),
                  reads=[b_ps[tch % 4]], writes=[b_xgT], tag=f"xg.cp{tch}")

        NKH = (KD + KH - 1) // KH
        n_main = NST * 2

        def deq_stage_parts(sidx):
            s = sidx // 2
            m = "g" if sidx % 2 == 0 else "u"
            par = sidx % 2
            sp = s % 2
            dma_step("sync", [(pk_sb[m][sp][:], d[m + "qw"][s]),
                              (qzst[m][sp][:], d[m + "qz"][s])],
                     writes=[b_pk[m][sp], b_qz[m][sp]], tag=f"{m}{s}.pk")
            dma_step("gpsimd", [(ssch[m][sp][:], d[m + "sc"][s])],
                     writes=[b_ss[m][sp]], tag=f"{m}{s}.sc")

            def zero_part():
                def em_uz(e, m=m, sp=sp):
                    last = None
                    for i in range(8):
                        last = e.tensor_scalar(
                            znib[m][sp][0:KD, PI[i]:SJ:8], qzst[m][sp][:],
                            4 * i, 15, ALU.logical_shift_right,
                            ALU.bitwise_and)
                    return last
                P.add("vector", em_uz, reads=[b_qz[m][sp]],
                      writes=[b_zs[m][sp]], tag=f"{m}{s}.upz")
                P.add("vector",
                      lambda e, m=m, sp=sp: e.scalar_tensor_tensor(
                          nzs[m][sp][:], znib[m][sp][0:KD, :], -1.0,
                          ssch[m][sp][:], ALU.mult, ALU.mult),
                      reads=[b_zs[m][sp], b_ss[m][sp]],
                      writes=[b_nzs[m][sp]], tag=f"{m}{s}.nzs")

            GB = max(1, 512 // SJ)

            def half_part(kh):
                h0 = kh * KH
                h1 = min(KD, h0 + KH)

                def em_uw(e, m=m, h0=h0, h1=h1, sp=sp):
                    last = None
                    w3 = nibgu[:].rearrange("p (g j) -> p g j", g=KH)
                    for i in range(8):
                        last = e.tensor_scalar(
                            w3[:, 0:h1 - h0, PI[i]:SJ:8],
                            pk_sb[m][sp][:, h0 * PC:h1 * PC],
                            4 * i, 15, ALU.logical_shift_right,
                            ALU.bitwise_and)
                    return last
                P.add("vector", em_uw, reads=[b_pk[m][sp]],
                      writes=[b_nib], tag=f"{m}{s}.upw{kh}")

                for b0 in range(h0, h1, GB):
                    b1 = min(h1, b0 + GB)
                    bp = state["bc"]
                    state["bc"] = (bp + 1) % 4

                    def em_bc(e, b0=b0, b1=b1, m=m, bp=bp, sp=sp):
                        last = None
                        for g in range(b0, b1):
                            last = e.matmul(
                                p_s[bp][:, (g - b0) * SJ:(g - b0 + 1) * SJ],
                                hot32[0:KD, g * 128:(g + 1) * 128],
                                ssch[m][sp][0:KD, :], start=True, stop=True)
                        return last
                    P.add("tensor", em_bc,
                          reads=[b_h32, b_ss[m][sp]],
                          writes=[b_ps[bp]], tag=f"{m}{s}.bc{b0}")
                    nsl = nibgu[:, (b0 - h0) * SJ:(b1 - h0) * SJ]
                    sl = wstb[par][:, b0 * SJ:b1 * SJ]
                    P.add("vector",
                          lambda e, sl=sl, nsl=nsl, b0=b0, b1=b1, bp=bp:
                          e.tensor_tensor(
                              sl, nsl, p_s[bp][:, 0:(b1 - b0) * SJ],
                              ALU.mult),
                          reads=[b_ps[bp], b_nib, b_wst[par]],
                          writes=[b_wst[par]], tag=f"{m}{s}.mul{b0}")

            parts = [lambda: (zero_part(), half_part(0))]
            for kh in range(1, NKH):
                parts.append(lambda kh=kh: half_part(kh))
            return parts

        def mm_pr(sidx, pr):
            s = sidx // 2
            m = "g" if sidx % 2 == 0 else "u"
            par = sidx % 2
            sp = s % 2
            prw = min(1024, T - pr)

            def em_mm(e, m=m, par=par, pr=pr, prw=prw, sp=sp):
                last = None
                for k in range(KD):
                    for tci in range(0, prw, 512):
                        wdt = min(512, prw - tci)
                        last = e.matmul(
                            p_gu[m][:, tci:tci + wdt],
                            wstb[par][:, k * SJ:(k + 1) * SJ],
                            lnT3()[:, k, pr + tci:pr + tci + wdt],
                            start=(k == 0), stop=False)
                for tci in range(0, prw, 512):
                    wdt = min(512, prw - tci)
                    last = e.matmul(
                        p_gu[m][:, tci:tci + wdt], nzs[m][sp][:],
                        xgT[0:KD, pr + tci:pr + tci + wdt],
                        start=False, stop=True)
                return last
            P.add("tensor", em_mm,
                  reads=[b_wst[par], b_lnT, b_nzs[m][sp], b_xgT],
                  writes=[b_pgu[m]], tag=f"{m}{s}.mm{pr}")
            if m == "g":
                P.add("scalar",
                      lambda e, pr=pr, prw=prw: e.activation(
                          silu_sb[:, pr:pr + prw], p_gu["g"][:, 0:prw],
                          AF.Silu),
                      reads=[b_pgu["g"]], writes=[b_silu],
                      tag=f"{m}{s}.si{pr}")
            else:
                hp = s % 2
                P.add("vector",
                      lambda e, pr=pr, prw=prw, hp=hp:
                      e.scalar_tensor_tensor(
                          hTb[hp][:, pr:pr + prw], p_gu["u"][:, 0:prw],
                          1.0, silu_sb[:, pr:pr + prw],
                          ALU.mult, ALU.mult),
                      reads=[b_pgu["u"], b_silu], writes=[b_hTb[hp]],
                      tag=f"{m}{s}.hm{pr}")

        INTERLEAVE_DEQ = True
        prs = list(range(0, T, 1024))
        pending = deq_stage_parts(0)
        for p in pending:
            p()
        for sidx in range(n_main):
            s = sidx // 2
            m = "g" if sidx % 2 == 0 else "u"
            nxt = deq_stage_parts(sidx + 1) if sidx + 1 < n_main else []
            for pi, pr in enumerate(prs):
                mm_pr(sidx, pr)
                if INTERLEAVE_DEQ:
                    share = (len(nxt) * (pi + 1)) // max(1, len(prs))
                    while len(nxt) and share > 0:
                        nxt.pop(0)()
                        share -= 1
            for p in nxt:
                p()
            if m == "u":
                hp = s % 2
                dma_step("sync", [(d["hT"][s], hTb[hp][:])],
                         reads=[b_hTb[hp]], tag=f"hT.st{s}")
        P.barrier()

    outer_es.close()

    with contextlib.ExitStack() as es:
        wdn_sb = es.enter_context(
            nc.sbuf_tensor("wdn_sb", [128, JG * DNW], BF16))
        b_wg = [Buf(f"wdn{g}") for g in range(JG)]
        hT_sb = [es.enter_context(
            nc.sbuf_tensor(f"hT_sb{i}", [128, TPR * 128], BF16))
            for i in range(6)]
        b_hts = [Buf(f"hts{i}") for i in range(6)]
        x1t = [es.enter_context(
            nc.sbuf_tensor(f"x1t{i}", [128, DNW], F32)) for i in range(4)]
        b_x1t = [Buf(f"x1t{i}") for i in range(4)]
        ot_sb = [es.enter_context(
            nc.sbuf_tensor(f"ot_sb{i}", [128, DNW], F32)) for i in range(4)]
        b_ot = [Buf(f"ot{i}") for i in range(4)]
        DGB = 4
        dpk = [es.enter_context(
            nc.sbuf_tensor(f"dpk{i}", [128, DGB * (DNW // 8)], I32))
            for i in range(2)]
        dnib = [es.enter_context(
            nc.sbuf_tensor(f"dnib{i}", [128, DGB * DNW], I32))
            for i in range(2)]
        dssc = [es.enter_context(
            nc.sbuf_tensor(f"dssc{i}", [JG, DNW], BF16)) for i in range(2)]
        dzsc = [es.enter_context(
            nc.sbuf_tensor(f"dzsc{i}", [JG, DNW], BF16)) for i in range(2)]
        dznb = [es.enter_context(
            nc.sbuf_tensor(f"dznb{i}", [JG, DNW], I32)) for i in range(2)]
        dqzs = [es.enter_context(
            nc.sbuf_tensor(f"dqzs{i}", [JG, DNW // 8], I32))
            for i in range(2)]
        hotsl = [es.enter_context(
            nc.sbuf_tensor(f"hotsl{i}", [JG, 128], BF16)) for i in range(4)]
        b_hsl = [Buf(f"hsl{i}") for i in range(4)]
        b_dpk = [Buf("dpk0"), Buf("dpk1")]
        b_dnib = [Buf("dnib0"), Buf("dnib1")]
        b_dss = [Buf("dss0"), Buf("dss1")]
        b_dzs = [Buf("dzs0"), Buf("dzs1")]
        b_dznb = [Buf("dznb0"), Buf("dznb1")]
        b_dqz = [Buf("dqz0"), Buf("dqz1")]
        p_dn = [es.enter_context(
            nc.psum_tensor(f"p_dn{i}", [128, 512], F32)) for i in range(TPR)]
        b_pdn = [Buf(f"pdn{i}") for i in range(TPR)]
        p_zs = [es.enter_context(
            nc.psum_tensor(f"p_zs{i}", [128, 512], F32)) for i in range(4)]
        b_pzs = [Buf(f"pzs{i}") for i in range(4)]

        def emit_down_unit(nci, g):
            np_ = nci % 2
            if g == 0:
                dma_step("gpsimd", [(dssc[np_][:], d["dsc"][nci])],
                         writes=[b_dss[np_]], tag=f"dn.sc{nci}")
                dma_step("scalar", [(dqzs[np_][:], d["dqz"][nci])],
                         writes=[b_dqz[np_]], tag=f"dn.qz{nci}")

                def em_uz(e, np_=np_):
                    last = None
                    for i in range(8):
                        last = e.tensor_scalar(
                            dznb[np_][0:JG, PI[i]:DNW:8], dqzs[np_][:],
                            4 * i, 15, ALU.logical_shift_right,
                            ALU.bitwise_and)
                    return last
                P.add("vector", em_uz, reads=[b_dqz[np_]],
                      writes=[b_dznb[np_]], tag=f"dn.uz{nci}")
                P.add("vector",
                      lambda e, np_=np_: e.tensor_copy(
                          dzsc[np_][:], dznb[np_][0:JG, :]),
                      reads=[b_dznb[np_]], writes=[b_dzs[np_]],
                      tag=f"dn.cz{nci}")
            dp = (g // DGB) % 2
            if g % DGB == 0:
                g1 = min(JG, g + DGB)
                ng = g1 - g
                pcw = DNW // 8
                dma_step("scalar",
                         [(dpk[dp][:, 0:ng * pcw],
                           d["dqw"][nci, :, g * pcw:g1 * pcw])],
                         writes=[b_dpk[dp]], tag=f"dn.pk{nci}.{g}")

                def em_uw(e, ng=ng, pcw=pcw, dp=dp):
                    last = None
                    w3 = dnib[dp][:].rearrange("p (g w) -> p g w", g=DGB)
                    for i in range(8):
                        last = e.tensor_scalar(
                            w3[:, 0:ng, PI[i]:DNW:8],
                            dpk[dp][:, 0:ng * pcw],
                            4 * i, 15, ALU.logical_shift_right,
                            ALU.bitwise_and)
                    return last
                P.add("vector", em_uw, reads=[b_dpk[dp]],
                      writes=[b_dnib[dp]], tag=f"dn.uw{nci}.{g}")
            hp = g % 4
            dma_step("scalar",
                     [(hotsl[hp][:],
                       d["chotJG"][:, g * 128:(g + 1) * 128])],
                     writes=[b_hsl[hp]], tag=f"dn.hs{nci}.{g}")
            zp = 2 * (g % 2)

            def em_bc(e, hp=hp, zp=zp, np_=np_):
                e.matmul(p_zs[zp][:, 0:DNW], hotsl[hp][:],
                         dzsc[np_][0:JG, 0:DNW], start=True, stop=True)
                return e.matmul(p_zs[zp + 1][:, 0:DNW], hotsl[hp][:],
                                dssc[np_][0:JG, 0:DNW],
                                start=True, stop=True)
            P.add("tensor", em_bc,
                  reads=[b_hsl[hp], b_dzs[np_], b_dss[np_]],
                  writes=[b_pzs[zp], b_pzs[zp + 1]], tag=f"dn.bc{nci}.{g}")
            nsl = dnib[dp][:, (g % DGB) * DNW:(g % DGB + 1) * DNW]
            sl = wdn_sb[:, g * DNW:(g + 1) * DNW]
            P.add("vector",
                  lambda e, sl=sl, nsl=nsl, zp=zp: e.tensor_tensor(
                      sl, nsl, p_zs[zp][:, 0:DNW], ALU.subtract),
                  reads=[b_pzs[zp], b_dnib[dp]], writes=[b_wg[g]],
                  tag=f"dn.sub{nci}.{g}")
            P.add("vector",
                  lambda e, sl=sl, zp=zp: e.tensor_tensor(
                      sl, sl, p_zs[zp + 1][:, 0:DNW], ALU.mult),
                  reads=[b_pzs[zp + 1]], writes=[b_wg[g]],
                  tag=f"dn.mul{nci}.{g}")

        for g in range(JG):
            emit_down_unit(0, g)
        for nci in range(NC_DN):
            for tr in range(NTR):
                for g in range(JG):
                    par = g % 6
                    hs, bhs = hT_sb[par], b_hts[par]
                    dma_step("sync",
                             [(hs[:], d["hT"][g][:, tr * TPR * 128:
                                                 (tr + 1) * TPR * 128])],
                             writes=[bhs], tag=f"dm.h{nci}.{tr}.{g}")

                    def em_mm(e, g=g, hs=hs):
                        last = None
                        for ti in range(TPR):
                            last = e.matmul(
                                p_dn[ti][:, 0:DNW],
                                hs[:, ti * 128:(ti + 1) * 128],
                                wdn_sb[:, g * DNW:(g + 1) * DNW],
                                start=(g == 0), stop=(g == JG - 1))
                        return last
                    P.add("tensor", em_mm, reads=[bhs, b_wg[g]],
                          writes=b_pdn, tag=f"dm.mm{nci}.{tr}.{g}")
                    if tr == NTR - 1 and nci + 1 < NC_DN:
                        emit_down_unit(nci + 1, g)
                for ti in range(TPR):
                    tc = tr * TPR + ti
                    qpar = ti % 4
                    xs, bxs = x1t[qpar], b_x1t[qpar]
                    os_, bos = ot_sb[qpar], b_ot[qpar]
                    dma_step("scalar",
                             [(xs[:], d["x1"][tc * 128:(tc + 1) * 128,
                                              nci * DNW:(nci + 1) * DNW])],
                             writes=[bxs], tag=f"dm.x{nci}.{tc}")
                    P.add("vector",
                          lambda e, ti=ti, os_=os_, xs=xs: e.tensor_tensor(
                              os_[:], p_dn[ti][:, 0:DNW], xs[:], ALU.add),
                          reads=[b_pdn[ti], bxs], writes=[bos],
                          tag=f"dm.add{nci}.{tc}")
                    dma_step("sync",
                             [(d["out"][tc * 128:(tc + 1) * 128,
                                        nci * DNW:(nci + 1) * DNW], os_[:])],
                             reads=[bos], tag=f"dm.st{nci}.{tc}")


    finals = [(DMA_SEM[q] + str(r), 16 * P.dma_count[q][r])
              for q in DMA_SEM for r in range(NLANE)
              if P.dma_count[q][r] > 0]

    n_instr = P.emit_all(nc, sems)

    with nc.Block() as block:
        @block.sync
        def _(sync):
            for name, val in finals:
                sync.wait_ge(sems[name], val)

    nc.finalize()
    return nc, n_instr



def prep_common(cfg, Wq, bq, Wk, bk, Wv, bv, Wo, bo,
                gqw, gsc, gqz, uqw, usc, uqz, dqw, dsc, dqz):
    T, D, H, I = cfg["T"], cfg["D"], cfg["H"], cfg["I"]
    KD, NST, PC, SJ = cfg["KD"], cfg["NST"], cfg["PC"], cfg["SJ"]
    JG, NC_DN, DNW = cfg["JG"], cfg["NC_DN"], cfg["DNW"]

    def headify(W):
        WT = np.ascontiguousarray(W.T)
        r = WT.reshape(KD, 128, H, 128)
        return np.ascontiguousarray(
            r.transpose(2, 1, 0, 3).reshape(H, 128, KD * 128))

    def pack_qw(qw, nst, pc, ktiles):
        r = qw.reshape(ktiles, 128, nst, pc)
        return np.ascontiguousarray(
            r.transpose(2, 1, 0, 3).reshape(nst, 128, ktiles * pc))

    def pack_rows(a, nst, w, groups):
        r = a.reshape(groups, nst, w)
        return np.ascontiguousarray(r.transpose(1, 0, 2))

    common = {
        "wq": headify(Wq), "wk": headify(Wk), "wv": headify(Wv),
        "wo": np.ascontiguousarray(Wo.T).reshape(KD, 128, D),
        "bq": np.ascontiguousarray(bq.reshape(1, D)),
        "bk": np.ascontiguousarray(bk.reshape(1, D)),
        "bvc": np.ascontiguousarray(bv.reshape(D, 1)),
        "bo": np.ascontiguousarray(bo.reshape(1, D)),
        "gqw": pack_qw(gqw, NST, PC, KD),
        "gsc": pack_rows(gsc, NST, SJ, KD),
        "gqz": pack_rows(gqz, NST, PC, KD),
        "uqw": pack_qw(uqw, NST, PC, KD),
        "usc": pack_rows(usc, NST, SJ, KD),
        "uqz": pack_rows(uqz, NST, PC, KD),
        "dqw": pack_qw(dqw, NC_DN, DNW // 8, JG),
        "dsc": pack_rows(dsc, NC_DN, DNW, JG),
        "dqz": pack_rows(dqz, NC_DN, DNW // 8, JG),
    }
    common.update(host_consts(cfg))
    return common


def kernel(L0_attn_x, L0_attn_Wq, L0_attn_bq, L0_attn_Wk, L0_attn_bk,
           L0_attn_Wv, L0_attn_bv, L0_attn_Wo, L0_attn_bo,
           L0_ffn_gate_qweight, L0_ffn_gate_scales, L0_ffn_gate_qzeros,
           L0_ffn_up_qweight, L0_ffn_up_scales, L0_ffn_up_qzeros,
           L0_ffn_down_qweight, L0_ffn_down_scales, L0_ffn_down_qzeros):
    x = np.asarray(L0_attn_x, dtype=np.float32)
    B, T, D = x.shape
    cfg = make_cfg(T=T, D=D, H=32, I=11008)
    common = prep_common(
        cfg,
        np.asarray(L0_attn_Wq, np.float32), np.asarray(L0_attn_bq, np.float32),
        np.asarray(L0_attn_Wk, np.float32), np.asarray(L0_attn_bk, np.float32),
        np.asarray(L0_attn_Wv, np.float32), np.asarray(L0_attn_bv, np.float32),
        np.asarray(L0_attn_Wo, np.float32), np.asarray(L0_attn_bo, np.float32),
        np.asarray(L0_ffn_gate_qweight, np.int32),
        np.asarray(L0_ffn_gate_scales, np.float32),
        np.asarray(L0_ffn_gate_qzeros, np.int32),
        np.asarray(L0_ffn_up_qweight, np.int32),
        np.asarray(L0_ffn_up_scales, np.float32),
        np.asarray(L0_ffn_up_qzeros, np.int32),
        np.asarray(L0_ffn_down_qweight, np.int32),
        np.asarray(L0_ffn_down_scales, np.float32),
        np.asarray(L0_ffn_down_qzeros, np.int32))
    in_maps = []
    for c in range(B):
        m = dict(common)
        m["x"] = np.ascontiguousarray(x[c])
        in_maps.append(m)
    nc, _ = build_program(cfg)
    res = run_bass_kernel_spmd(nc, in_maps, core_ids=list(range(B)))
    out = np.stack([np.asarray(r["out"]) for r in res.results], axis=0)
    return out.astype(np.float32)
